# version 16
# baseline (speedup 1.0000x reference)
import numpy as np
import ml_dtypes
import concourse.bass as bass
import concourse.mybir as mybir

F32 = mybir.dt.float32
BF16 = mybir.dt.bfloat16
AF = mybir.ActivationFunctionType
ALU = mybir.AluOpType
AX = mybir.AxisListType

MAGIC = float(np.float32(1.5 * 2.0**23))
C, H, D, L, NCC = 1024, 16, 64, 32, 8
QMAX = 32767.0
EPS = 1e-5

ENGS = ("T", "V", "A", "G", "S")
DMAQ = {"S": "Sq", "A": "Aq", "G": "Gq"}
NDMASEM = 8


class KB:
    def __init__(self, nc):
        self.nc = nc
        self.prog = {e: [] for e in ENGS}
        self.cnt = {e: 0 for e in ENGS}
        self.dcnt = {f"{q}{j}": 0 for q in DMAQ.values() for j in range(NDMASEM)}
        self.didx = {q: 0 for q in DMAQ.values()}
        self.last_w = {}
        self.all_w = {}
        self.snaps = {}
        self.readers = {}
        self.waited = {e: {} for e in ENGS}
        self.sems = {}

    @staticmethod
    def key(x):
        if isinstance(x, str):
            return x
        t = getattr(x, "tensor", x)
        return t.name

    def _want(self, eng, tok):
        sem, val = tok
        w = self.waited[eng]
        if w.get(sem, 0) >= val:
            return None
        w[sem] = val
        snap = self.snaps.get(tok)
        if snap:
            for s2, v2 in snap.items():
                if w.get(s2, 0) < v2:
                    w[s2] = v2
        return tok

    def op(self, eng, fn, reads=(), writes=(), dma=False, tag="", skip_waw=False):
        rkeys = [self.key(x) for x in reads]
        wkeys = [self.key(x) for x in writes]
        waits = []
        for k in rkeys:
            for t in self.all_w.get(k, {}).items():
                t2 = self._want(eng, t)
                if t2:
                    waits.append(t2)
        for k in wkeys:
            if not skip_waw:
                for t in self.all_w.get(k, {}).items():
                    t2 = self._want(eng, t)
                    if t2:
                        waits.append(t2)
            for t in self.readers.get(k, ()):
                t2 = self._want(eng, t)
                if t2:
                    waits.append(t2)
        if dma:
            q = DMAQ[eng]
            j = self.didx[q] % NDMASEM
            self.didx[q] += 1
            qs = f"{q}{j}"
            prev = self.dcnt[qs]
            if prev > 0:
                t2 = self._want(eng, (qs, prev))
                if t2:
                    waits.append(t2)
            self.dcnt[qs] = prev + 16
            tok = (qs, self.dcnt[qs])
        else:
            self.cnt[eng] += 1
            tok = (eng, self.cnt[eng])
        snap = dict(self.waited[eng])
        snap[eng] = self.cnt[eng]
        if dma:
            snap[tok[0]] = tok[1]
        self.snaps[tok] = snap
        for k in rkeys:
            self.readers.setdefault(k, []).append(tok)
        for k in wkeys:
            self.last_w[k] = tok
            d = self.all_w.setdefault(k, {})
            d[tok[0]] = max(d.get(tok[0], 0), tok[1])
            self.readers[k] = []
        self.prog[eng].append((waits, fn, tok, tag))
        return tok

    def barrier(self):
        toks = [(e, self.cnt[e]) for e in ENGS if self.cnt[e] > 0]
        toks += [(qs, v) for qs, v in self.dcnt.items() if v > 0]
        for e in ENGS:
            waits = []
            for t in toks:
                t2 = self._want(e, t)
                if t2:
                    waits.append(t2)
            self.cnt[e] += 1
            self.prog[e].append((waits, None, (e, self.cnt[e]), "barrier"))

    def realize(self, block):
        nc = self.nc
        for name in list(ENGS) + list(self.dcnt):
            self.sems[name] = nc.alloc_semaphore("sem_" + name)
        deco = {"T": block.tensor, "V": block.vector, "A": block.scalar,
                "G": block.gpsimd, "S": block.sync}

        def run_stream(eng, ename):
            for waits, fn, tok, tag in self.prog[ename]:
                for s, v in waits:
                    eng.wait_ge(self.sems[s], v)
                sem, _ = tok
                ins = eng.nop() if fn is None else fn(eng)
                ins.then_inc(self.sems[sem], 16 if sem in self.dcnt else 1)

        for ename in ENGS:
            if self.prog[ename]:
                deco[ename](lambda eng, en=ename: run_stream(eng, en))

    def dma(self, eng, out, in_, reads=(), writes=(), tag=""):
        return self.op(eng, lambda e: e.dma_start(out=out, in_=in_),
                       reads=reads, writes=writes, dma=True, tag=tag)

    @staticmethod
    def _free_dims(ap_obj):
        try:
            pairs = [(int(p[0]), int(p[1])) for p in ap_obj.ap]
        except Exception:
            return 1
        free = [(st, sz) for st, sz in pairs[1:] if sz > 1]
        if not free:
            return 1
        n, cst, csz = 1, free[-1][0], free[-1][1]
        for st, sz in reversed(free[:-1]):
            if st == cst * csz:
                csz *= sz
            else:
                n += 1
                cst, csz = st, sz
        return n

    def _ck(self, ap_obj, what, tag):
        if self._free_dims(ap_obj) > 1:
            raise RuntimeError(f"matmul {what} has multi-dim free AP at {tag}: "
                               f"{[list(p) for p in ap_obj.ap]}")

    def mm(self, out, lhsT, rhs, start=True, stop=True, reads=(), writes=(),
           tp=None, tag=""):
        self._ck(lhsT, "lhsT", tag); self._ck(rhs, "rhs", tag)
        return self.op("T", lambda e: e.matmul(out, lhsT, rhs, start=start,
                                               stop=stop, tile_position=tp),
                       reads=reads, writes=writes, tag=tag,
                       skip_waw=not start)

    def trans(self, out, in_, ident, reads=(), writes=(), tp=None, tag=""):
        self._ck(in_, "trans_in", tag)
        return self.op("T", lambda e: e.transpose(out, in_, ident,
                                                  tile_position=tp),
                       reads=reads, writes=writes, tag=tag)

    def tt(self, eng, out, in0, in1, alu, reads=(), writes=(), tag=""):
        return self.op(eng, lambda e: e.tensor_tensor(out=out, in0=in0,
                                                      in1=in1, op=alu),
                       reads=reads, writes=writes, tag=tag)

    def ts(self, eng, out, in0, s1, s2, op0, op1=None, reads=(), writes=(), tag=""):
        if s2 is None:
            return self.op(eng, lambda e: e.tensor_single_scalar(
                out=out, in_=in0, scalar=s1, op=op0),
                reads=reads, writes=writes, tag=tag)
        return self.op(eng, lambda e: e.tensor_scalar(
            out=out, in0=in0, scalar1=s1, scalar2=s2, op0=op0, op1=op1),
            reads=reads, writes=writes, tag=tag)

    def stt(self, eng, out, in0, scalar, in1, op0, op1, reads=(), writes=(), tag=""):
        return self.op(eng, lambda e: e.scalar_tensor_tensor(
            out=out, in0=in0, scalar=scalar, in1=in1, op0=op0, op1=op1),
            reads=reads, writes=writes, tag=tag)

    def act(self, out, in_, func, bias=0.0, scale=1.0, reads=(), writes=(), tag=""):
        return self.op("A", lambda e: e.activation(out=out, in_=in_, func=func,
                                                   bias=bias, scale=scale),
                       reads=reads, writes=writes, tag=tag)

    def red(self, eng, out, in_, alu, absval=False, reads=(), writes=(), tag=""):
        return self.op(eng, lambda e: e.tensor_reduce(
            out=out, in_=in_, axis=AX.X, op=alu, apply_absolute_value=absval),
            reads=reads, writes=writes, tag=tag)

    def copy(self, eng, out, in_, reads=(), writes=(), tag=""):
        if eng == "A":
            return self.op(eng, lambda e: e.copy(out, in_),
                           reads=reads, writes=writes, tag=tag)
        return self.op(eng, lambda e: e.tensor_copy(out=out, in_=in_),
                       reads=reads, writes=writes, tag=tag)

    def scan_op(self, eng, out, d0, d1, init, op0, op1, reads=(), writes=(), tag=""):
        return self.op(eng, lambda e: e.tensor_tensor_scan(
            out=out, data0=d0, data1=d1, initial=init, op0=op0, op1=op1),
            reads=reads, writes=writes, tag=tag)

    def memset(self, eng, ap, val, writes=(), tag=""):
        return self.op(eng, lambda e: e.memset(ap, val), writes=writes, tag=tag)

    def recip(self, out, in_, reads=(), writes=(), tag=""):
        return self.op("V", lambda e: e.reciprocal(out=out, in_=in_),
                       reads=reads, writes=writes, tag=tag)


VR = dict(x_r=0, x_w=1, x_k=2, x_v=3, x_a=4, k_k=5, k_a=6, r_k=7,
          w_lora_bias=8, a_lora_bias=9, g_lora_bias=10, gn_weight=11, gn_bias=12)


def build_program(T=2048, W=256, n_cores=8, taps=(), upto="full", ffn_prep=True, only_head=None):
    nc = bass.Bass("TRN2", target_bir_lowering=False)
    NW, NCH = T // W, W // L
    NCG, NT = NCH // 4, W // 128
    W1 = W + 1
    NI = 3072
    NSH = NI // n_cores // 128
    HSH = C // n_cores

    dram = {}
    def din(name, shape, dt=F32):
        dram[name] = nc.dram_tensor(name, shape, dt, kind="ExternalInput").ap()
        return dram[name]
    xT_d = din("xT", [C, T])
    for w in ("W_r", "W_k", "W_v", "W_o"):
        din(w, [C, C])
    din("w_lora_a", [C, 64]); din("w_lora_b", [64, C])
    din("a_lora_a", [C, 64]); din("a_lora_b", [64, C])
    din("g_lora_a", [C, 128]); din("g_lora_b", [128, C])
    din("vecs", [C, 13]); din("vrows", [13, C])
    din("ffn_up_sh", [NI // n_cores, C])
    din("ffn_gate_sh", [NI // n_cores, C])
    din("ffn_down_sh", [HSH, NI])
    y_d = nc.dram_tensor("y", [C, T], F32, kind="ExternalOutput").ap()
    tap_d = {}
    for tname, tshape, tdt in taps:
        tap_d[tname] = nc.dram_tensor("tap_" + tname, tshape, tdt,
                                      kind="ExternalOutput").ap()

    xres_d = nc.dram_tensor("xres_d", [C, T], F32, kind="Internal").ap()
    up_sp_d = nc.dram_tensor("up_spill", [NI, T], BF16, kind="Internal").ap()
    wupT_sh = nc.dram_tensor("wupT_sh", [C, NI // n_cores], BF16, kind="Internal").ap()
    wgtT_sh = nc.dram_tensor("wgtT_sh", [C, NI // n_cores], BF16, kind="Internal").ap()
    wdnT_sh = nc.dram_tensor("wdnT_sh", [NI, HSH], BF16, kind="Internal").ap()
    wupT_g = nc.dram_tensor("wupT_g", [n_cores, C, NI // n_cores], BF16, kind="Internal").ap()
    wgtT_g = nc.dram_tensor("wgtT_g", [n_cores, C, NI // n_cores], BF16, kind="Internal").ap()
    wdnT_g = nc.dram_tensor("wdnT_g", [n_cores, NI, HSH], BF16, kind="Internal").ap()
    mx_in = nc.dram_tensor("mx_in", [8], F32, kind="Internal").ap()
    scal_d = nc.dram_tensor("scal_d", [8], F32, kind="Internal").ap()
    mx_out = nc.dram_tensor("mx_out", [n_cores, 8], F32, kind="Internal").ap()
    _FFN_DRAM.clear()
    _FFN_DRAM.update(wupg=wupT_g, mx_out_d=mx_out, xres_dd=xres_d)

    bfd = ml_dtypes.bfloat16
    eye = lambda n: np.eye(n, dtype=np.float32)
    rp = np.ones((128, W), np.float32); rp[:, ::L] = 0.0
    CT = {}
    CT["rp"] = nc.inline_tensor(rp, "c_rp")
    CT["id64d"] = nc.inline_tensor(np.vstack([eye(64), eye(64)]).astype(bfd), "c_id64d")
    CT["id16"] = nc.inline_tensor(eye(16).astype(bfd), "c_id16")
    CT["id128"] = nc.inline_tensor(eye(128).astype(bfd), "c_id128")
    CT["id128f"] = nc.inline_tensor(eye(128), "c_id128f")
    CT["magt"] = nc.inline_tensor(np.full((128, 1), 1.5 * 2.0**23, np.float32), "c_magt")
    CT["i64st"] = nc.inline_tensor(np.vstack([eye(64), eye(64)]).astype(np.float32), "c_i64st")
    def quadmask(k):
        m = np.zeros((128, 128), np.float32)
        for j in range(4):
            blk = (np.tril(np.ones((32, 32)), -1) if k == -1 else
                   np.triu(np.ones((32, 32)), 1) if k == 1 else
                   np.triu(np.ones((32, 32)), 0))
            m[32 * j:32 * j + 32, 32 * j:32 * j + 32] = blk
        return m.astype(np.float32)
    CT["mtril"] = nc.inline_tensor(quadmask(-1).astype(bfd), "c_mtril")
    CT["mtriu"] = nc.inline_tensor(quadmask(+1).astype(bfd), "c_mtriu")
    CT["mtriu0"] = nc.inline_tensor(quadmask(0).astype(bfd), "c_mtriu0")
    CT["ones"] = nc.inline_tensor(np.ones((128, 1), bfd), "c_ones")
    CT["cst"] = nc.inline_tensor(np.tile(np.array([[0.0, 1.0, EPS]], np.float32), (128, 1)), "c_cst")
    CT["onesr"] = nc.inline_tensor(np.ones((1, 128), np.float32), "c_onesr")
    for cc in range(NCC):
        m = np.zeros((128, 16), np.float32)
        m[0:64, 2 * cc] = 1.0; m[64:128, 2 * cc + 1] = 1.0
        CT[f"hsum{cc}"] = nc.inline_tensor(m.astype(bfd), f"c_hsum{cc}")
        bm = np.zeros((16, 128), np.float32)
        bm[2 * cc, 0:64] = 1.0; bm[2 * cc + 1, 64:128] = 1.0
        CT[f"hbc{cc}"] = nc.inline_tensor(bm.astype(bfd), f"c_hbc{cc}")

    b = KB(nc)
    A = {}
    regions = {"D": None, "F": None}
    rcur = {}
    import math as _math
    def _dtb(dt):
        return 2 if dt == BF16 else 4
    def sb(name, shape, dt, region=None):
        if region is None:
            A[name] = nc.alloc_sbuf_tensor("s_" + name, shape, dt)
            return A[name]
        nbytes = int(np.prod(shape[1:])) * _dtb(dt)
        off = rcur[region]
        off = (off + 31) // 32 * 32
        rcur[region] = off + nbytes
        assert rcur[region] <= nc.SBUF_PARTITION_SIZE_BYTES, \
            (name, region, rcur[region])
        A[name] = nc.alloc_sbuf_tensor_at("s_" + name, shape, dt, offset=off)
        return A[name]

    fbase = nc.SBUF_PARTITION_SIZE_BYTES - nc.sbuf_bytes_remaining
    sb("Wr", [128, NCC, C], BF16); sb("Wk", [128, NCC, C], BF16)
    sb("Wv", [128, NCC, C], BF16); sb("Wo", [128, NCC, C], BF16)
    sb("wla", [128, NCC, 64], BF16); sb("wlb", [64, C], BF16)
    sb("ala", [128, NCC, 64], BF16); sb("alb", [64, C], BF16)
    sb("gla", [128, NCC, 128], BF16); sb("glb", [128, C], BF16)
    sb("vecs", [128, NCC, 13], F32)
    sb("vaux", [128, 2, NCC], F32)
    sb("gnw_t", [128, C], BF16); sb("gnb_t", [128, C], BF16)
    sb("gbias_t", [128, C], BF16)
    sb("id64d", [128, 64], BF16); sb("id16", [16, 16], BF16)
    sb("id128", [128, 128], BF16)
    sb("i64st", [128, 64], F32)
    sb("mtril", [128, 128], BF16); sb("mtriu", [128, 128], BF16)
    sb("mtriu0", [128, 128], BF16)
    sb("rp", [128, W], F32)
    for cc in range(NCC):
        sb(f"hsum{cc}", [128, 16], BF16); sb(f"hbc{cc}", [16, 128], BF16)
    wbase = nc.SBUF_PARTITION_SIZE_BYTES - nc.sbuf_bytes_remaining
    sb("xb0", [128, NCC, W1], BF16); sb("xb1", [128, NCC, W1], BF16)
    sb("xf0", [128, NCC, W], F32); sb("xf1", [128, NCC, W], F32)
    sb("xrnew", [128, NCC, W], F32)
    sb("hw", [128, NCC, W1], BF16)
    sb("dl", [128, NCC, W], BF16)
    sb("mixA", [128, NCC, W], BF16)
    sb("mixB", [128, NCC, W], BF16)
    sb("lnsA", [128, NCC, W1], BF16)
    sb("sq", [128, NCC, W1], BF16)
    sb("stat", [1, 5, 256], F32)
    sb("rw", [128, NCC, W], BF16)
    sb("kw", [128, NCC, W], BF16)
    sb("vw", [128, NCC, W], BF16)
    sb("aw", [128, NCC, W], BF16)
    sb("f32a", [128, NCC, W], F32)
    sb("f32b", [128, NCC, W], F32)
    sb("logw", [128, NCC, W], F32)
    sb("gw", [128, NCC, W], F32)
    sb("eg", [128, NCC, W], BF16)
    sb("emg", [128, NCC, W], BF16)
    sb("egp", [128, NCC, W], BF16)
    sb("lam0", [128, NCC, NCH], F32); sb("lam1", [128, NCC, NCH], F32)
    sb("lamp0", [128, NCC, NCH], F32); sb("lamp1", [128, NCC, NCH], F32)
    sb("k2", [128, NCC, W], BF16)
    sb("kkw", [128, NCC, W], BF16)
    sb("braw", [128, NCC, W], BF16)
    sb("rkp", [128, NCC, W], BF16)
    sb("m1", [128, NCC, W], BF16)
    sb("kn16", [16, W], F32); sb("kn16b", [16, W], F32)
    sb("kinvb", [16, W], BF16)
    sb("beta", [16, W], BF16)
    sb("betat0", [128, NT, 16], BF16); sb("betat1", [128, NT, 16], BF16)
    sb("th64", [64, W], BF16)
    sb("sg", [128, W], BF16)
    sb("ggate0", [128, NT, C], BF16); sb("ggate1", [128, NT, C], BF16)
    sb("SC", [128, NCC, 4, NCH, L], BF16)
    sb("Vtm", [128, NT, C], BF16)
    sb("Otm", [128, NT, C], F32)
    for par in range(2):
        sb(f"Tq{par}", [128, 128], BF16); sb(f"Aq{par}", [128, 128], BF16)
        sb(f"Mq{par}", [128, 128], BF16); sb(f"Kq{par}", [128, 128], BF16)
        sb(f"Bq{par}", [128, 128], BF16)
        sb(f"P2{par}", [128, 256], BF16); sb(f"P4{par}", [128, 256], BF16)
        sb(f"P8{par}", [128, 256], BF16); sb(f"P16{par}", [128, 128], BF16)
        sb(f"rhs{par}", [128, 128], BF16)
        sb(f"Zb{par}", [128, 128], BF16)
        sb(f"Zc{par}", [128, 128], BF16)
    for par in range(4):
        sb(f"kbv{par}", [128, 192], BF16)
        sb(f"Za{par}", [128, 128], BF16)
        sb(f"OI{par}", [128, 64], F32)
    for eqp in range(4):
        sb(f"EQt{eqp}", [128, 384], BF16)
        sb(f"EQ{eqp}", [128, 384], BF16)
    sb("Ybuf", [128, NCC, 64], BF16)
    sb("Ysc", [128, 64], F32)
    sb("goTM", [128, NT, C], BF16)
    sb("goFM", [128, NCC, W], BF16)
    sb("attw", [128, NCC, W], BF16)
    sb("gn16", [128, NT, 16], F32); sb("gn16b", [128, NT, 16], F32)
    sb("gnA", [128, NT, 16], F32); sb("gnB", [128, NT, 16], F32)
    sb("scal", [128, 8], F32)
    sb("sm", [1, 64], F32)
    sb("red1", [128, 8], F32)
    sb("ones", [128, 1], BF16); sb("onesr", [1, 128], F32)
    sb("cst", [128, 3], F32)
    rcur["D"] = wbase
    sb("fsc1", [128, 2048], F32, region="D")
    sb("fsc2", [128, 2048], F32, region="D")
    sb("fsh", [128, max(NSH, 1), C], F32, region="D")
    sb("fshT", [128, NCC, NSH * 128], BF16, region="D")
    sb("dqs", [128, 96], F32, region="D"); sb("dqr", [128, 96], F32, region="D")
    sb("dsh", [128, NI], F32, region="D")
    sb("dshT", [128, NI // 128, HSH], BF16, region="D")
    sb("wqtmp", [128, C], BF16, region="D")
    sb("dq3", [128, NI], BF16, region="D")

    rcur["F"] = fbase
    sb("ffw", [128, 2, NCC, 128], BF16, region="F")
    sb("gateb", [128, 24, T], BF16, region="F")
    sb("lnf", [128, NCC, T], BF16, region="F")
    sb("fout", [128, NCC, 256], F32, region="F")
    sb("fres", [128, NCC, 256], F32, region="F")
    sb("fsf1", [128, max(T, 2048)], F32, region="F")
    sb("xbf", [128, NCC, 256], BF16, region="F")
    sb("sqf", [128, NCC, 256], BF16, region="F")
    sb("sc1f", [128, NCC, 256], BF16, region="F")
    sb("id128f", [128, 128], F32, region="F")
    sb("magt", [128, 1], F32, region="F")
    sb("smg", [1, 64], F32, region="F")
    upk_off = rcur["F"]
    sb("upk", [128, 2, T], BF16, region="F")
    sb("fsf2", [128, max(T, 2048)], F32, region="F")
    end_off = rcur["F"]
    rcur["F"] = upk_off
    sb("dnw", [128, 2, 24, 128], BF16, region="F")
    rcur["F"] = max(rcur["F"], end_off)
    sb("statf", [1, 5, 256], F32, region="F")
    sb("fsf1b", [128, max(T, 2048)], F32, region="F")
    sb("upkb", [128, max(T, 2048)], BF16, region="F")
    assert rcur["F"] <= nc.SBUF_PARTITION_SIZE_BYTES, rcur["F"]
    PS = {i: nc.alloc_psum_tensor(f"pb{i}", [128, 512], F32) for i in range(7)}
    pT = nc.alloc_psum_tensor("pT", [128, 1024], BF16)

    rem = nc.sbuf_bytes_remaining
    print(f"[build] sbuf remaining {rem} bytes/partition")

    Vc = lambda r, cc: A["vecs"][:, cc, r:r + 1]
    bc3 = lambda ap2, n: ap2.rearrange("p (cc u) -> p cc u", u=1).broadcast_to([128, NCC, n])

    for nm in ("rp", "id64d", "id16", "id128", "i64st", "mtril", "mtriu",
               "mtriu0", "ones", "onesr", "cst"):
        b.dma("S", A[nm][:], CT[nm].ap()[:], writes=[A[nm]])
    for cc in range(NCC):
        b.dma("S", A[f"hsum{cc}"][:], CT[f"hsum{cc}"].ap()[:], writes=[A[f"hsum{cc}"]])
        b.dma("S", A[f"hbc{cc}"][:], CT[f"hbc{cc}"].ap()[:], writes=[A[f"hbc{cc}"]])
    b.dma("S", A["vecs"][:], dram["vecs"].rearrange("(cc p) r -> p cc r", p=128),
          writes=[A["vecs"]])
    for nm, row in [("gnw_t", "gn_weight"), ("gnb_t", "gn_bias"),
                    ("gbias_t", "g_lora_bias")]:
        b.dma("G", A[nm][:], dram["vrows"][VR[row]][None, :].broadcast_to([128, C]),
              writes=[A[nm]])
    nc.const_aps.register(F32, 0.0, A["cst"][:, 0:1])
    nc.const_aps.register(F32, 1.0, A["cst"][:, 1:2])
    nc.const_aps.register(F32, EPS, A["cst"][:, 2:3])
    for i in range(7):
        b.memset("V", PS[i][:], 0.0, writes=[PS[i]], tag="pz")
    b.barrier()
    b.tt("V", A["vaux"][:, 0, :], A["vecs"][:, :, VR["k_k"]],
         A["vecs"][:, :, VR["k_k"]], ALU.mult, reads=[A["vecs"]], writes=[A["vaux"]])
    b.ts("V", A["vaux"][:, 1, :], A["vecs"][:, :, VR["k_a"]], -1.0, 1.0,
         ALU.mult, ALU.add, reads=[A["vecs"]], writes=[A["vaux"]])
    for nm, d in [("Wr", "W_r"), ("Wk", "W_k"), ("Wv", "W_v"), ("Wo", "W_o")]:
        b.dma("G", A[nm][:], dram[d].rearrange("(cc p) j -> p cc j", p=128),
              writes=[A[nm]])
    for nm, d in [("wla", "w_lora_a"), ("ala", "a_lora_a"), ("gla", "g_lora_a")]:
        b.dma("G", A[nm][:], dram[d].rearrange("(cc p) r -> p cc r", p=128),
              writes=[A[nm]])
    for nm, d in [("wlb", "w_lora_b"), ("alb", "a_lora_b"), ("glb", "g_lora_b")]:
        b.dma("G", A[nm][:], dram[d][:], writes=[A[nm]])
    yk = [f"Y{h}" for h in range(H)]
    b.memset("V", A["Ybuf"][:], 0.0, writes=[A["Ybuf"]] + yk)

    def dq_rows(view, nb, out_bf):
        n = 32 * nb
        b.red("V", A["dqs"][:, 0:nb], view.rearrange("p (n k) -> p n k", k=32),
              ALU.max, absval=True, reads=[view], writes=[A["dqs"]])
        b.ts("V", A["dqs"][:, 0:nb], A["dqs"][:, 0:nb], 1.0 / 7.0, 1e-12,
             ALU.mult, ALU.max, reads=[A["dqs"]], writes=[A["dqs"]])
        b.recip(A["dqr"][:, 0:nb], A["dqs"][:, 0:nb], reads=[A["dqs"]],
                writes=[A["dqr"]])
        sbc = A["dqs"][:, 0:nb].rearrange("p (n u) -> p n u", u=1).broadcast_to([128, nb, 32])
        rbc = A["dqr"][:, 0:nb].rearrange("p (n u) -> p n u", u=1).broadcast_to([128, nb, 32])
        b.tt("V", A["fsc1"][:, 0:n].rearrange("p (n k) -> p n k", k=32),
             view.rearrange("p (n k) -> p n k", k=32), rbc, ALU.mult,
             reads=[view, A["dqr"]], writes=[A["fsc1"]])
        b.ts("V", A["fsc2"][:, 0:n], A["fsc1"][:, 0:n], MAGIC, MAGIC,
             ALU.add, ALU.subtract, reads=[A["fsc1"]], writes=[A["fsc2"]])
        b.tt("V", out_bf.rearrange("p (n k) -> p n k", k=32),
             A["fsc2"][:, 0:n].rearrange("p (n k) -> p n k", k=32), sbc, ALU.mult,
             reads=[A["fsc2"], A["dqs"]], writes=[out_bf])

    def collective(outs_ap, ins_ap, reads=(), writes=()):
        return b.op("G", lambda e: e.collective_compute(
            "AllGather", ALU.bypass, [list(range(n_cores))], [ins_ap], [outs_ap]),
            reads=reads, writes=writes, tag="AG")

    for src, sh_d in ([("ffn_up_sh", wupT_sh), ("ffn_gate_sh", wgtT_sh)] if ffn_prep else []):
        b.dma("S", A["fsh"][:, 0:NSH, :],
              dram[src].rearrange("(ic p) c -> p ic c", p=128), writes=[A["fsh"]])
        for ic in range(NSH):
            dq_rows(A["fsh"][:, ic, :], 32, A["wqtmp"][:])
            for cc in range(NCC):
                b.trans(pT[:, 0:128], A["wqtmp"][:, 128 * cc:128 * cc + 128],
                        A["id128"][:], reads=[A["wqtmp"], A["id128"]],
                        writes=[pT], tag="dqT")
                b.copy("V", A["fshT"][:, cc, 128 * ic:128 * ic + 128], pT[:, 0:128],
                       reads=[pT], writes=[A["fshT"]])
        b.dma("A", sh_d.rearrange("(cc p) i -> p cc i", p=128), A["fshT"][:],
              reads=[A["fshT"]], writes=[KB.key(sh_d)])
    if ffn_prep:
        collective(wupT_g[:], wupT_sh[:], reads=["wupT_sh"], writes=["wupT_g"])
        collective(wgtT_g[:], wgtT_sh[:], reads=["wgtT_sh"], writes=["wgtT_g"])

        b.dma("S", A["dsh"][:], dram["ffn_down_sh"].rearrange("(u p) i -> p u i", p=HSH)
              [:, 0, :] if HSH == 128 else dram["ffn_down_sh"][:], writes=[A["dsh"]])
        dq_rows(A["dsh"][:, 0:2048], 64, A["dq3"][:, 0:2048])
        dq_rows(A["dsh"][:, 2048:NI], 32, A["dq3"][:, 2048:NI])
        for ic in range(NI // 128):
            srcv = A["dq3"][:, 128 * ic:128 * ic + 128]
            b.trans(pT[:, 0:128], srcv, A["id128"][:],
                    reads=[A["dq3"], A["id128"]], writes=[pT])
            b.copy("V", A["dshT"][:, ic, :], pT[:, 0:HSH], reads=[pT], writes=[A["dshT"]])
        b.dma("A", wdnT_sh.rearrange("(ic p) h -> p ic h", p=128), A["dshT"][:],
              reads=[A["dshT"]], writes=["wdnT_sh"])
        collective(wdnT_g[:], wdnT_sh[:], reads=["wdnT_sh"], writes=["wdnT_g"])
    b.barrier()

    def ln_apply(xin, n_t, out, sq_buf, scr1, scr2, inv_bank, nmi_bank,
                 extra=None, tag="ln", stat_name="stat"):
        st = A[stat_name]
        b.act(sq_buf, xin, AF.Square, reads=[xin], writes=[sq_buf], tag=tag + "sq")
        for cc in range(NCC):
            b.mm(PS[6][0:1, 0:n_t], A["ones"][:], xin[:, cc, :],
                 start=(cc == 0), stop=(cc == NCC - 1),
                 reads=[A["ones"], xin], writes=[PS[6]], tag=tag)
        b.ts("V", st[:, 2, 0:n_t], PS[6][0:1, 0:n_t], 1.0 / C, None, ALU.mult,
             reads=[PS[6]], writes=[st], tag=tag)
        for cc in range(NCC):
            b.mm(PS[6][0:1, 0:n_t], A["ones"][:], sq_buf[:, cc, :],
                 start=(cc == 0), stop=(cc == NCC - 1),
                 reads=[A["ones"], sq_buf], writes=[PS[6]], tag=tag)
        b.ts("V", st[:, 1, 0:n_t], PS[6][0:1, 0:n_t], 1.0 / C, None, ALU.mult,
             reads=[PS[6]], writes=[st], tag=tag)
        b.tt("V", st[:, 0, 0:n_t], st[:, 2, 0:n_t], st[:, 2, 0:n_t], ALU.mult,
             reads=[st], writes=[st], tag=tag)
        b.tt("V", st[:, 1, 0:n_t], st[:, 1, 0:n_t], st[:, 0, 0:n_t], ALU.subtract,
             reads=[st], writes=[st], tag=tag)
        b.act(st[:, 0, 0:n_t], st[:, 1, 0:n_t], AF.Sqrt, bias=EPS,
              reads=[st], writes=[st], tag=tag)
        b.recip(st[:, 3, 0:n_t], st[:, 0, 0:n_t], reads=[st], writes=[st])
        b.stt("V", st[:, 4, 0:n_t], st[:, 2, 0:n_t], -1.0, st[:, 3, 0:n_t],
              ALU.mult, ALU.mult, reads=[st], writes=[st], tag=tag)
        b.mm(inv_bank, A["onesr"][:], st[:, 3, 0:n_t], start=True, stop=True,
             reads=[A["onesr"], st], writes=[inv_bank], tag=tag)
        b.mm(nmi_bank, A["onesr"][:], st[:, 4, 0:n_t], start=True, stop=True,
             reads=[A["onesr"], st], writes=[nmi_bank], tag=tag)
        ib = inv_bank.rearrange("p (u t) -> p u t", u=1).broadcast_to([128, NCC, n_t])
        nb = nmi_bank.rearrange("p (u t) -> p u t", u=1).broadcast_to([128, NCC, n_t])
        b.tt("V", scr1, xin, ib, ALU.mult, reads=[xin, inv_bank], writes=[scr1],
             tag=tag)
        if extra is None:
            b.tt("V", out, scr1, nb, ALU.add, reads=[scr1, nmi_bank], writes=[out],
                 tag=tag)
        else:
            b.tt("V", scr2, scr1, nb, ALU.add, reads=[scr1, nmi_bank], writes=[scr2],
                 tag=tag)
            b.tt("V", out, scr2, extra, ALU.add, reads=[scr2, extra], writes=[out],
                 tag=tag)

    def fm_proj(out_buf, w_buf, rhs_buf, evac, tag, nb=3):
        for jj in range(NCC):
            pb = PS[jj % nb]
            for cc in range(NCC):
                b.mm(pb[:, 0:W], w_buf[:, cc, 128 * jj:128 * jj + 128],
                     rhs_buf[:, cc, 0:W], start=(cc == 0), stop=(cc == NCC - 1),
                     reads=[w_buf, rhs_buf], writes=[pb], tag=tag)
            evac(jj, pb)

    def emit_loads(w):
        t0 = w * W
        wk = f"w{w}"
        xbb = A[f"xb{w % 2}"]
        if w == 0:
            b.dma("G", xbb[:, :, 1:W1],
                  xT_d.rearrange("(cc p) t -> p cc t", p=128)[:, :, 0:W],
                  writes=[xbb], tag=wk)
            b.memset("V", xbb[:, :, 0:1], 0.0, writes=[xbb], tag=wk)
        else:
            b.dma("G", xbb[:],
                  xT_d.rearrange("(cc p) t -> p cc t", p=128)[:, :, t0 - 1:t0 + W],
                  writes=[xbb], tag=wk)
        b.dma("S", A[f"xf{w % 2}"][:],
              xT_d.rearrange("(cc p) t -> p cc t", p=128)[:, :, t0:t0 + W],
              writes=[A[f"xf{w % 2}"]], tag=wk)

    def emit_prep(w):
        t0 = w * W
        wk = f"w{w}"
        xbb = A[f"xb{w % 2}"]
        ln_apply(xbb[:], W1, A["hw"][:], A["sq"][:], A["lnsA"][:], None,
                 PS[5][:, 0:W1], PS[4][:, 0:W1], tag=wk + "ln1")
        if w == 0:
            b.memset("V", A["hw"][:, :, 0:1], 0.0, writes=[A["hw"]], tag=wk)
        b.tt("V", A["dl"][:], A["hw"][:, :, 0:W], A["hw"][:, :, 1:W1],
             ALU.subtract, reads=[A["hw"]], writes=[A["dl"]], tag=wk + "dl")

        def mix_into(buf, vrow, eng):
            vb = A["vecs"][:, :, vrow].rearrange("p (cc u) -> p cc u", u=1)\
                .broadcast_to([128, NCC, W])
            scr = A["f32a"][:] if eng == "V" else A["f32b"][:]
            b.tt(eng, scr, A["dl"][:], vb, ALU.mult,
                 reads=[A["dl"], A["vecs"]], writes=[scr], tag=wk + "mx")
            b.tt(eng, buf, scr, A["hw"][:, :, 1:W1], ALU.add,
                 reads=[scr, A["hw"]], writes=[buf], tag=wk + "mx")

        def evac_copy(buf):
            def f(jj, pb):
                b.ts("V", buf[:, jj, 0:W], pb[:, 0:W], 0.0, None, ALU.add,
                     reads=[pb], writes=[buf], tag=wk + "pe")
            return f

        mix_into(A["mixA"][:], VR["x_r"], "V")
        fm_proj(A["rw"], A["Wr"], A["mixA"], evac_copy(A["rw"]), wk + "pr")
        mix_into(A["mixB"][:], VR["x_k"], "G")
        fm_proj(A["kw"], A["Wk"], A["mixB"], evac_copy(A["kw"]), wk + "pk")
        mix_into(A["mixA"][:], VR["x_v"], "V")
        fm_proj(A["vw"], A["Wv"], A["mixA"], evac_copy(A["vw"]), wk + "pv")

        mix_into(A["mixB"][:], VR["x_w"], "G")
        for cc in range(NCC):
            b.mm(PS[2][0:64, 0:W], A["wla"][:, cc, :], A["mixB"][:, cc, :],
                 start=(cc == 0), stop=(cc == NCC - 1),
                 reads=[A["wla"], A["mixB"]], writes=[PS[2]], tag=wk + "wl1")
        b.act(A["th64"][:, 0:W], PS[2][0:64, 0:W], AF.Tanh, reads=[PS[2]],
              writes=[A["th64"]], tag=wk + "wl1e")
        for jj in range(NCC):
            pb = PS[jj % 2]
            b.mm(pb[:, 0:W], A["wlb"][0:64, 128 * jj:128 * jj + 128],
                 A["th64"][0:64, 0:W], start=True, stop=True,
                 reads=[A["wlb"], A["th64"]], writes=[pb], tag=wk + "wl2")
            b.act(A["f32a"][:, jj, :], pb[:, 0:W], AF.Identity,
                  bias=Vc(VR["w_lora_bias"], jj), reads=[pb, A["vecs"]],
                  writes=[A["f32a"]], tag=wk + "wl2e")
        b.act(A["f32b"][:], A["f32a"][:], AF.Exp, scale=-1.0, reads=[A["f32a"]],
              writes=[A["f32b"]], tag=wk + "sp")
        b.act(A["f32a"][:], A["f32b"][:], AF.Ln, bias=1.0, reads=[A["f32b"]],
              writes=[A["f32a"]], tag=wk + "sp")
        b.ts("V", A["logw"][:], A["f32a"][:], -1.0, -0.5, ALU.mult, ALU.add,
             reads=[A["f32a"]], writes=[A["logw"]], tag=wk + "sp")
        mix_into(A["mixB"][:], VR["x_a"], "G")
        for cc in range(NCC):
            b.mm(PS[2][0:64, 0:W], A["ala"][:, cc, :], A["mixB"][:, cc, :],
                 start=(cc == 0), stop=(cc == NCC - 1),
                 reads=[A["ala"], A["mixB"]], writes=[PS[2]], tag=wk + "al1")
        b.copy("A", A["th64"][:, 0:W], PS[2][0:64, 0:W], reads=[PS[2]],
               writes=[A["th64"]], tag=wk + "al1e")
        for jj in range(NCC):
            pb = PS[jj % 2]
            b.mm(pb[:, 0:W], A["alb"][0:64, 128 * jj:128 * jj + 128],
                 A["th64"][0:64, 0:W], start=True, stop=True,
                 reads=[A["alb"], A["th64"]], writes=[pb], tag=wk + "al2")
            b.act(A["aw"][:, jj, :], pb[:, 0:W], AF.Sigmoid,
                  bias=Vc(VR["a_lora_bias"], jj), reads=[pb, A["vecs"]],
                  writes=[A["aw"]], tag=wk + "al2e")
        for cc in range(NCC):
            b.scan_op("V", A["gw"][:, cc, :], A["rp"][:], A["logw"][:, cc, :], 0.0,
                      ALU.mult, ALU.add, reads=[A["rp"], A["logw"]],
                      writes=[A["gw"]], tag=wk + "gs")
        b.tt("G", A["f32a"][:], A["gw"][:], A["logw"][:], ALU.subtract,
             reads=[A["gw"], A["logw"]], writes=[A["f32a"]], tag=wk + "gp")
        b.act(A["eg"][:], A["gw"][:], AF.Exp, reads=[A["gw"]], writes=[A["eg"]],
              tag=wk + "eg")
        b.act(A["emg"][:], A["gw"][:], AF.Exp, scale=-1.0, reads=[A["gw"]],
              writes=[A["emg"]], tag=wk + "emg")
        b.act(A["egp"][:], A["f32a"][:], AF.Exp, reads=[A["f32a"]],
              writes=[A["egp"]], tag=wk + "egp")
        b.act(A[f"lam{w % 2}"][:],
              A["gw"][:].rearrange("p cc (ch l) -> p cc ch l", l=L)[:, :, :, L - 1],
              AF.Exp, reads=[A["gw"]], writes=[A[f"lam{w % 2}"]], tag=wk + "lam")
        b.copy("V", A[f"lamp{w % 2}"][:, :, 1:NCH], A[f"lam{w % 2}"][:, :, 0:NCH - 1],
               reads=[A[f"lam{w % 2}"]], writes=[A[f"lamp{w % 2}"]], tag=wk + "lp")
        b.memset("V", A[f"lamp{w % 2}"][:, :, 0:1], 1.0, writes=[A[f"lamp{w % 2}"]], tag=wk + "lp")
        kab = bc3(A["vecs"][:, :, VR["k_a"]], W)
        okab = bc3(A["vaux"][:, 1, :], W)
        b.tt("G", A["f32b"][:], A["aw"][:], kab, ALU.mult,
             reads=[A["aw"], A["vecs"]], writes=[A["f32b"]], tag=wk + "m1")
        b.tt("G", A["m1"][:], A["f32b"][:], okab, ALU.add,
             reads=[A["f32b"], A["vaux"]], writes=[A["m1"]], tag=wk + "m1")
        b.tt("V", A["k2"][:], A["kw"][:], A["m1"][:], ALU.mult,
             reads=[A["kw"], A["m1"]], writes=[A["k2"]], tag=wk + "k2")
        b.act(A["sq"][:, :, 0:W], A["kw"][:], AF.Square, reads=[A["kw"]],
              writes=[A["sq"]], tag=wk + "ks")
        b.tt("V", A["mixB"][:], A["sq"][:, :, 0:W], bc3(A["vaux"][:, 0, :], W),
             ALU.mult, reads=[A["sq"], A["vaux"]], writes=[A["mixB"]], tag=wk + "ks")
        for cc in range(NCC):
            b.mm(PS[2][0:16, 0:W], A[f"hsum{cc}"][:], A["mixB"][:, cc, :],
                 start=(cc == 0), stop=(cc == NCC - 1),
                 reads=[A[f"hsum{cc}"], A["mixB"]], writes=[PS[2]], tag=wk + "kn")
        b.act(A["kn16"][:, 0:W], PS[2][0:16, 0:W], AF.Sqrt, reads=[PS[2]],
              writes=[A["kn16"]], tag=wk + "kn")
        b.ts("V", A["kn16b"][:, 0:W], A["kn16"][:, 0:W], 1e-12, None, ALU.max,
             reads=[A["kn16"]], writes=[A["kn16b"]], tag=wk + "kn")
        b.recip(A["kn16"][:, 0:W], A["kn16b"][:, 0:W], reads=[A["kn16b"]],
                writes=[A["kn16"]])
        b.copy("V", A["kinvb"][:, 0:W], A["kn16"][:, 0:W], reads=[A["kn16"]],
               writes=[A["kinvb"]], tag=wk + "kn")
        for cc in range(NCC):
            b.mm(PS[6][:, 0:W], A[f"hbc{cc}"][:], A["kinvb"][0:16, 0:W],
                 start=True, stop=True, reads=[A[f"hbc{cc}"], A["kinvb"]],
                 writes=[PS[6]], tag=wk + "kb")
            b.stt("V", A["kkw"][:, cc, :], A["kw"][:, cc, :], Vc(VR["k_k"], cc),
                  PS[6][:, 0:W], ALU.mult, ALU.mult,
                  reads=[A["kw"], A["vecs"], PS[6]], writes=[A["kkw"]], tag=wk + "kk")
        b.tt("V", A["braw"][:], A["kkw"][:], A["aw"][:], ALU.mult,
             reads=[A["kkw"], A["aw"]], writes=[A["braw"]], tag=wk + "br")
        for cc in range(NCC):
            b.stt("V", A["rkp"][:, cc, :], A["rw"][:, cc, :], Vc(VR["r_k"], cc),
                  A["k2"][:, cc, :], ALU.mult, ALU.mult,
                  reads=[A["rw"], A["vecs"], A["k2"]], writes=[A["rkp"]],
                  tag=wk + "rk")
        for cc in range(NCC):
            b.mm(PS[2][0:16, 0:W], A[f"hsum{cc}"][:], A["rkp"][:, cc, :],
                 start=(cc == 0), stop=(cc == NCC - 1),
                 reads=[A[f"hsum{cc}"], A["rkp"]], writes=[PS[2]], tag=wk + "bt")
        b.copy("V", A["beta"][:, 0:W], PS[2][0:16, 0:W], reads=[PS[2]],
               writes=[A["beta"]], tag=wk + "bt")
        for tt in range(NT):
            b.trans(pT[:, 512:528], A["beta"][0:16, 128 * tt:128 * tt + 128],
                    A["id16"][:], reads=[A["beta"], A["id16"]], writes=[pT],
                    tag=wk + "btT")
            b.copy("V", A[f"betat{w % 2}"][:, tt, :], pT[:, 512:528], reads=[pT],
                   writes=[A[f"betat{w % 2}"]], tag=wk + "btT")
        for cc in range(NCC):
            b.mm(PS[4][:, 0:W], A["gla"][:, cc, :], A["rw"][:, cc, :],
                 start=(cc == 0), stop=(cc == NCC - 1),
                 reads=[A["gla"], A["rw"]], writes=[PS[4]], tag=wk + "g1")
        b.act(A["sg"][:, 0:W], PS[4][:, 0:W], AF.Sigmoid, reads=[PS[4]],
              writes=[A["sg"]], tag=wk + "g1e")
        for tt in range(NT):
            for hf in range(2):
                pb = PS[hf]
                b.mm(pb[:, 0:512], A["sg"][:, 128 * tt:128 * tt + 128],
                     A["glb"][:, 512 * hf:512 * hf + 512], start=True, stop=True,
                     reads=[A["sg"], A["glb"]], writes=[pb], tag=wk + "g2")
                b.tt("V", A[f"ggate{w % 2}"][:, tt, 512 * hf:512 * hf + 512], pb[:, 0:512],
                     A["gbias_t"][:, 512 * hf:512 * hf + 512], ALU.add,
                     reads=[pb, A["gbias_t"]], writes=[A[f"ggate{w % 2}"]], tag=wk + "g2e")
        scv = lambda t: A["SC"][:, :, t, :, :]
        shp = lambda x: x.rearrange("p cc (ch l) -> p cc ch l", l=L)
        b.tt("V", scv(0), shp(A["braw"][:]), shp(A["emg"][:]), ALU.mult,
             reads=[A["braw"], A["emg"]], writes=[A["SC"]], tag=wk + "sc")
        b.tt("V", scv(1), shp(A["k2"][:]), shp(A["emg"][:]), ALU.mult,
             reads=[A["k2"], A["emg"]], writes=[A["SC"]], tag=wk + "sc")
        b.tt("G", scv(2), shp(A["rw"][:]), shp(A["eg"][:]), ALU.mult,
             reads=[A["rw"], A["eg"]], writes=[A["SC"]], tag=wk + "sc")
        b.stt("V", scv(3), shp(A["kkw"][:]), -1.0, shp(A["egp"][:]), ALU.mult,
              ALU.mult, reads=[A["kkw"], A["egp"]], writes=[A["SC"]], tag=wk + "sc")

    def emit_scan(w):
        wk = f"w{w}"
        if only_head is not None:
            okeys0 = [f"Otm_{ch}_{hh}" for ch in range(NCH) for hh in range(H)]
            vkeys0 = [f"Vtm_{cg}_{hh}" for cg in range(NCG) for hh in range(H)]
            b.memset("V", A["Otm"][:], 0.0, writes=[A["Otm"]] + okeys0, tag=wk)
            b.memset("V", A["Vtm"][:], 0.0, writes=[A["Vtm"]] + vkeys0, tag=wk)
            for cg in range(NCG):
                emit_quad(b, A, PS, pT, cg, only_head, wk, w % 2)
                for ch in range(4 * cg, 4 * cg + 4):
                    emit_seq(b, A, PS, ch, only_head, NCH, wk, w % 2)
            return
        import os as _os2
        MODE = _os2.environ.get("SCANMODE", "SER")
        if MODE.startswith("SER") and MODE != "SER":
            G = int(MODE[3:])
            hg = 0
            while hg < H:
                hs_grp = list(range(hg, min(hg + G, H)))
                for cg in range(NCG):
                    for hh in hs_grp:
                        emit_quad(b, A, PS, pT, cg, hh, wk, w % 2)
                    for ch in range(4 * cg, 4 * cg + 4):
                        for hh in hs_grp:
                            emit_seq(b, A, PS, ch, hh, NCH, wk, w % 2)
                hg += G
            return
        if MODE == "SER":
            for h0 in range(H):
                for cg in range(NCG):
                    emit_quad(b, A, PS, pT, cg, h0, wk, w % 2)
                    for ch in range(4 * cg, 4 * cg + 4):
                        emit_seq(b, A, PS, ch, h0, NCH, wk, w % 2)
        elif MODE == "QP":
            for h0 in range(0, H, 2):
                for cg in range(NCG):
                    emit_quad(b, A, PS, pT, cg, h0, wk, w % 2)
                    emit_quad(b, A, PS, pT, cg, h0 + 1, wk, w % 2)
                    for ch in range(4 * cg, 4 * cg + 4):
                        emit_seq(b, A, PS, ch, h0, NCH, wk, w % 2)
                    for ch in range(4 * cg, 4 * cg + 4):
                        emit_seq(b, A, PS, ch, h0 + 1, NCH, wk, w % 2)
        elif MODE.startswith("QPX"):
            skip = int(MODE[3:])
            for h0 in range(0, H, 2):
                for cg in range(NCG):
                    if h0 // 2 != skip:
                        emit_quad(b, A, PS, pT, cg, h0, wk, w % 2)
                        emit_quad(b, A, PS, pT, cg, h0 + 1, wk, w % 2)
                        for ch in range(4):
                            emit_seq(b, A, PS, ch, h0, NCH, wk, w % 2)
                        for ch in range(4):
                            emit_seq(b, A, PS, ch, h0 + 1, NCH, wk, w % 2)
                    else:
                        for hh in (h0, h0 + 1):
                            emit_quad(b, A, PS, pT, cg, hh, wk, w % 2)
                            for ch in range(4 * cg, 4 * cg + 4):
                                emit_seq(b, A, PS, ch, hh, NCH, wk, w % 2)
        elif MODE.startswith("QPO"):
            which = int(MODE[3:])
            for h0 in range(0, H, 2):
                for cg in range(NCG):
                    if h0 // 2 == which:
                        emit_quad(b, A, PS, pT, cg, h0, wk, w % 2)
                        emit_quad(b, A, PS, pT, cg, h0 + 1, wk, w % 2)
                        for ch in range(4):
                            emit_seq(b, A, PS, ch, h0, NCH, wk, w % 2)
                        for ch in range(4):
                            emit_seq(b, A, PS, ch, h0 + 1, NCH, wk, w % 2)
                    else:
                        for hh in (h0, h0 + 1):
                            emit_quad(b, A, PS, pT, cg, hh, wk, w % 2)
                            for ch in range(4 * cg, 4 * cg + 4):
                                emit_seq(b, A, PS, ch, hh, NCH, wk, w % 2)
        elif MODE.startswith("QPN"):
            npair = int(MODE[3:])
            for h0 in range(0, H, 2):
                for cg in range(NCG):
                    if h0 // 2 < npair:
                        emit_quad(b, A, PS, pT, cg, h0, wk, w % 2)
                        emit_quad(b, A, PS, pT, cg, h0 + 1, wk, w % 2)
                        for ch in range(4):
                            emit_seq(b, A, PS, ch, h0, NCH, wk, w % 2)
                        for ch in range(4):
                            emit_seq(b, A, PS, ch, h0 + 1, NCH, wk, w % 2)
                    else:
                        for hh in (h0, h0 + 1):
                            emit_quad(b, A, PS, pT, cg, hh, wk, w % 2)
                            for ch in range(4 * cg, 4 * cg + 4):
                                emit_seq(b, A, PS, ch, hh, NCH, wk, w % 2)
        else:
            for h0 in range(0, H, 2):
                for cg in range(NCG):
                    emit_quad(b, A, PS, pT, cg, h0, wk, w % 2)
                    emit_quad(b, A, PS, pT, cg, h0 + 1, wk, w % 2)
                    for ch in range(4 * cg, 4 * cg + 4):
                        emit_seq(b, A, PS, ch, h0, NCH, wk, w % 2)
                        emit_seq(b, A, PS, ch, h0 + 1, NCH, wk, w % 2)
    def emit_post(w):
        t0 = w * W
        wk = f"w{w}"
        ov = A["Otm"][:].rearrange("p tt (hh d) -> p tt hh d", d=64)
        okeys = [f"Otm_{ch}_{h}" for ch in range(NCH) for h in range(H)]
        vkeys = [f"Vtm_{cg}_{h}" for cg in range(NCG) for h in range(H)]
        b.red("V", A["gn16"][:], ov, ALU.add, reads=[A["Otm"]] + okeys,
              writes=[A["gn16"]], tag=wk + "gn")
        b.act(A["goTM"][:], A["Otm"][:], AF.Square, reads=[A["Otm"]] + okeys,
              writes=[A["goTM"]], tag=wk + "gn")
        b.red("V", A["gn16b"][:],
              A["goTM"][:].rearrange("p tt (hh d) -> p tt hh d", d=64), ALU.add,
              reads=[A["goTM"]], writes=[A["gn16b"]], tag=wk + "gn")
        b.ts("V", A["gn16"][:], A["gn16"][:], 1.0 / 64, None, ALU.mult,
             reads=[A["gn16"]], writes=[A["gn16"]], tag=wk + "gn")
        b.ts("V", A["gn16b"][:], A["gn16b"][:], 1.0 / 64, None, ALU.mult,
             reads=[A["gn16b"]], writes=[A["gn16b"]], tag=wk + "gn")
        b.tt("V", A["gnA"][:], A["gn16"][:], A["gn16"][:], ALU.mult,
             reads=[A["gn16"]], writes=[A["gnA"]], tag=wk + "gn")
        b.tt("V", A["gnB"][:], A["gn16b"][:], A["gnA"][:], ALU.subtract,
             reads=[A["gn16b"], A["gnA"]], writes=[A["gnB"]], tag=wk + "gn")
        b.act(A["gnB"][:], A["gnB"][:], AF.Sqrt, bias=EPS, reads=[A["gnB"]],
              writes=["gnBs"], tag=wk + "gn")
        b.recip(A["gnA"][:], A["gnB"][:], reads=["gnBs"], writes=[A["gnA"]])
        b.stt("V", A["gnB"][:], A["gn16"][:], -1.0, A["gnA"][:], ALU.mult, ALU.mult,
              reads=[A["gn16"], A["gnA"]], writes=[A["gnB"]], tag=wk + "gn")
        gA = A["gnA"][:].rearrange("p tt (hh u) -> p tt hh u", u=1)\
            .broadcast_to([128, NT, 16, 64])
        gB = A["gnB"][:].rearrange("p tt (hh u) -> p tt hh u", u=1)\
            .broadcast_to([128, NT, 16, 64])
        gv = A["goTM"][:].rearrange("p tt (hh d) -> p tt hh d", d=64)
        b.tt("V", gv, ov, gA, ALU.mult, reads=[A["Otm"], A["gnA"]],
             writes=[A["goTM"]], tag=wk + "gna")
        b.tt("V", A["mixA"][:].rearrange("p cc w -> p cc w") if False else
             A["goTM"][:].rearrange("p tt (hh d) -> p tt hh d", d=64),
             gv, gB, ALU.add, reads=[A["goTM"], A["gnB"]], writes=[A["goTM"]],
             tag=wk + "gna2")
        gnwb = A["gnw_t"][:].rearrange("p (u c) -> p u c", u=1).broadcast_to([128, NT, C])
        gnbb = A["gnb_t"][:].rearrange("p (u c) -> p u c", u=1).broadcast_to([128, NT, C])
        b.tt("G", A["goTM"][:], A["goTM"][:], gnwb, ALU.mult,
             reads=[A["goTM"], A["gnw_t"]], writes=[A["goTM"]], tag=wk + "gnw")
        b.tt("G", A["goTM"][:], A["goTM"][:], gnbb, ALU.add,
             reads=[A["goTM"], A["gnb_t"]], writes=[A["goTM"]], tag=wk + "gnb")
        bb = A[f"betat{w % 2}"][:].rearrange("p tt (hh u) -> p tt hh u", u=1)\
            .broadcast_to([128, NT, 16, 64])
        bsc = A["mixB"][:].rearrange("p cc w -> p (cc w)")\
            .rearrange("p (tt hh d) -> p tt hh d", tt=NT, d=64)
        b.tt("V", bsc, A["Vtm"][:].rearrange("p tt (hh d) -> p tt hh d", d=64),
             bb, ALU.mult,
             reads=[A["Vtm"], A[f"betat{w % 2}"]] + vkeys, writes=[A["mixB"]],
             tag=wk + "bo")
        b.tt("V", A["goTM"][:].rearrange("p tt (hh d) -> p tt hh d", d=64),
             A["goTM"][:].rearrange("p tt (hh d) -> p tt hh d", d=64), bsc, ALU.add,
             reads=[A["goTM"], A["mixB"]], writes=[A["goTM"]], tag=wk + "bo")
        b.tt("V", A["goTM"][:], A["goTM"][:], A[f"ggate{w % 2}"][:], ALU.mult,
             reads=[A["goTM"], A[f"ggate{w % 2}"]], writes=[A["goTM"]], tag=wk + "ga")
        for tt in range(NT):
            for cc in range(NCC):
                b.trans(pT[:, 640:768], A["goTM"][:, tt, 128 * cc:128 * cc + 128],
                        A["id128"][:], reads=[A["goTM"], A["id128"]], writes=[pT],
                        tag=wk + "gT")
                b.copy("A", A["goFM"][:, cc, 128 * tt:128 * tt + 128], pT[:, 640:768],
                       reads=[pT], writes=[A["goFM"]], tag=wk + "gT")
        def evac_att(jj, pb):
            b.copy("V" if jj % 2 == 0 else "A", A["attw"][:, jj, 0:W], pb[:, 0:W],
                   reads=[pb], writes=[A["attw"]], tag=wk + "woe")
        fm_proj(A["attw"], A["Wo"], A["goFM"], evac_att, wk + "wo")
        ln_apply(A["attw"][:], W, A["xrnew"][:], A["sq"][:, :, 0:W],
                 A["lnsA"][:, :, 0:W], A["mixA"][:], PS[5][:, 0:W], PS[4][:, 0:W],
                 extra=A[f"xf{w % 2}"][:], tag=wk + "al")
        b.dma("A", xres_d.rearrange("(cc p) t -> p cc t", p=128)[:, :, t0:t0 + W],
              A["xrnew"][:], reads=[A["xrnew"]], writes=["xres_dram"], tag=wk + "xr")

    if False:
        for w in range(NW):
            emit_prep(w)
            emit_scan(w)
            emit_post(w)
    else:
        emit_loads(0)
        emit_prep(0)
        for w in range(NW):
            emit_scan(w)
            if w + 1 < NW:
                emit_loads(w + 1)
                emit_prep(w + 1)
            emit_post(w)

    b.barrier()
    if upto == "attn":
        for blk in range(max(T // 256, 1)):
            BS = min(256, T)
            c0 = blk * BS
            b.dma("G", A["fres"][:, :, 0:BS],
                  xres_d.rearrange("(cc p) t -> p cc t", p=128)[:, :, c0:c0 + BS],
                  reads=["xres_dram"], writes=[A["fres"]], tag="out")
            b.dma("S", y_d.rearrange("(cc p) t -> p cc t", p=128)[:, :, c0:c0 + BS],
                  A["fres"][:, :, 0:BS], reads=[A["fres"]], writes=["y_out"], tag="out")
    else:
        ffn_phase(b, A, PS, pT, T, n_cores, NSH, HSH, dram, CT, xres_d, up_sp_d,
                  wupT_g, wgtT_g, wdnT_g, mx_in, mx_out, scal_d, y_d, collective,
                  ln_apply)
    for tname in tap_d:
        src_ap, rd = TAPS[tname](A)
        b.dma("S", tap_d[tname], src_ap, reads=rd, writes=[f"tap_{tname}"])
    b.barrier()
    with nc.Block() as block:
        b.realize(block)
    return nc, A, dram, tap_d


def emit_quad(b, A, PS, pT, cg, h, wk, wpar=0):
    cc, half = h // 2, 64 * (h % 2)
    par = (cg * 16 + h) % 2
    par4 = h % 4
    eqp = (cg * 16 + h) % 4
    p2 = h % 2
    SCORE, AUX, PW = PS[4 * p2], PS[1 + 4 * p2], PS[2 + 4 * p2]
    tg = f"{wk}q{cg}h{h}"
    SCs = lambda ty, ch: A["SC"][half:half + 64, cc, ty, ch, :]
    SC4 = lambda ty: A["SC"][half:half + 64, cc, ty, 4 * cg:4 * cg + 4, :]\
        .rearrange("p a b -> p (a b)")
    b.mm(SCORE[:, 0:128], SC4(3), SC4(0), start=True, stop=True, tp=(half, 0),
         reads=[A["SC"]], writes=[SCORE], tag=tg + "T")
    b.mm(SCORE[:, 128:256], SC4(0), SC4(3), start=True, stop=True, tp=(half, 0),
         reads=[A["SC"]], writes=[SCORE], tag=tg + "A")
    b.mm(SCORE[:, 256:384], SC4(1), SC4(3), start=True, stop=True, tp=(half, 0),
         reads=[A["SC"]], writes=[SCORE], tag=tg + "M")
    b.mm(SCORE[:, 384:512], SC4(1), SC4(2), start=True, stop=True, tp=(half, 0),
         reads=[A["SC"]], writes=[SCORE], tag=tg + "K")
    b.mm(AUX[:, 0:128], SC4(0), SC4(2), start=True, stop=True, tp=(half, 0),
         reads=[A["SC"]], writes=[AUX], tag=tg + "B")
    b.tt("V", A[f"Tq{par}"][:], SCORE[:, 0:128], A["mtril"][:], ALU.mult,
         reads=[SCORE, A["mtril"]], writes=[A[f"Tq{par}"]], tag=tg + "e")
    b.tt("V", A[f"Aq{par}"][:], SCORE[:, 128:256], A["mtriu"][:], ALU.mult,
         reads=[SCORE, A["mtriu"]], writes=[A[f"Aq{par}"]], tag=tg + "e")
    b.tt("V", A[f"Mq{par}"][:], SCORE[:, 256:384], A["mtriu"][:], ALU.mult,
         reads=[SCORE, A["mtriu"]], writes=[A[f"Mq{par}"]], tag=tg + "e")
    b.tt("V", A[f"Kq{par}"][:], SCORE[:, 384:512], A["mtriu0"][:], ALU.mult,
         reads=[SCORE, A["mtriu0"]], writes=[A[f"Kq{par}"]], tag=tg + "e")
    b.tt("V", A[f"Bq{par}"][:], AUX[:, 0:128], A["mtriu0"][:], ALU.mult,
         reads=[AUX, A["mtriu0"]], writes=[A[f"Bq{par}"]], tag=tg + "e")
    pb0 = 256 * p2
    ptk = f"pT{p2}"
    idd = A["id64d"][half:half + 64, :]
    b.trans(pT[:, pb0 + 0:pb0 + 64], SC4(1), idd, tp=(half, 0),
            reads=[A["SC"], A["id64d"]], writes=[ptk], tag=tg + "t")
    b.trans(pT[:, pb0 + 64:pb0 + 128], SC4(0), idd, tp=(half, 0),
            reads=[A["SC"], A["id64d"]], writes=[ptk], tag=tg + "t")
    b.trans(pT[:, pb0 + 128:pb0 + 192], A["vw"][half:half + 64, cc, 128 * cg:128 * cg + 128],
            idd, tp=(half, 0), reads=[A["vw"], A["id64d"]], writes=[ptk], tag=tg + "t")
    b.trans(pT[:, pb0 + 192:pb0 + 256], SC4(3), idd, tp=(half, 0),
            reads=[A["SC"], A["id64d"]], writes=[ptk], tag=tg + "t")
    b.copy("A", A[f"kbv{par4}"][:], pT[:, pb0 + 0:pb0 + 192], reads=[ptk],
           writes=[A[f"kbv{par4}"]], tag=tg + "kb")
    b.copy("A", A["Vtm"][:, cg, 64 * h:64 * h + 64], pT[:, pb0 + 128:pb0 + 192],
           reads=[ptk], writes=[f"Vtm_{cg}_{h}"], tag=tg + "vt")
    b.copy("A", A[f"rhs{par}"][:, 0:64], pT[:, pb0 + 192:pb0 + 256], reads=[ptk],
           writes=[A[f"rhs{par}"]], tag=tg + "rh")
    b.mm(AUX[:, 128:192], A[f"Mq{par}"][:], A[f"kbv{par4}"][:, 128:192],
         start=True, stop=True, reads=[A[f"Mq{par}"], A[f"kbv{par4}"]],
         writes=[AUX], tag=tg + "MV")
    b.copy("V", A[f"rhs{par}"][:, 64:128], AUX[:, 128:192], reads=[AUX],
           writes=[A[f"rhs{par}"]], tag=tg + "rh2")
    zs = [A[f"rhs{par}"], A[f"Za{par4}"], A[f"Zb{par}"], A[f"Zc{par}"],
          A[f"rhs{par}"], A[f"Za{par4}"]]
    def apply_i(i, pow_ap, pow_reads):
        b.mm(SCORE[:, 0:128], pow_ap, zs[i][:], start=True, stop=True,
             reads=pow_reads + [zs[i]], writes=[SCORE], tag=tg + f"a{i}")
        b.tt("V", zs[i + 1][:], SCORE[:, 0:128], zs[i][:], ALU.add,
             reads=[SCORE, zs[i]], writes=[zs[i + 1]], tag=tg + f"a{i}e")
    NOINTER = True
    pend = []
    def apply_d(i, pow_ap, pow_reads):
        if NOINTER:
            pend.append((i, pow_ap, pow_reads))
        else:
            apply_i(i, pow_ap, pow_reads)
    apply_d(0, A[f"Aq{par}"][:], [A[f"Aq{par}"]])
    b.mm(PW[:, 0:128], A[f"Aq{par}"][:], A[f"Tq{par}"][:], start=True, stop=True,
         reads=[A[f"Aq{par}"], A[f"Tq{par}"]], writes=[PW], tag=tg + "p2")
    b.mm(PW[:, 128:256], A[f"Tq{par}"][:], A[f"Aq{par}"][:], start=True, stop=True,
         reads=[A[f"Aq{par}"], A[f"Tq{par}"]], writes=[PW], tag=tg + "p2")
    b.ts("V", A[f"P2{par}"][:], PW[:, 0:256], 0.0, None, ALU.add,
         reads=[PW], writes=[A[f"P2{par}"]], tag=tg + "p2e")
    apply_d(1, A[f"P2{par}"][:, 128:256], [A[f"P2{par}"]])
    b.mm(PW[:, 256:384], A[f"P2{par}"][:, 128:256], A[f"P2{par}"][:, 0:128],
         start=True, stop=True, reads=[A[f"P2{par}"]], writes=[PW], tag=tg + "p4")
    b.mm(PW[:, 384:512], A[f"P2{par}"][:, 0:128], A[f"P2{par}"][:, 128:256],
         start=True, stop=True, reads=[A[f"P2{par}"]], writes=[PW], tag=tg + "p4")
    b.ts("V", A[f"P4{par}"][:], PW[:, 256:512], 0.0, None, ALU.add,
         reads=[PW], writes=[A[f"P4{par}"]], tag=tg + "p4e")
    apply_d(2, A[f"P4{par}"][:, 128:256], [A[f"P4{par}"]])
    b.mm(PW[:, 0:128], A[f"P4{par}"][:, 128:256], A[f"P4{par}"][:, 0:128],
         start=True, stop=True, reads=[A[f"P4{par}"]], writes=[PW], tag=tg + "p8")
    b.mm(PW[:, 128:256], A[f"P4{par}"][:, 0:128], A[f"P4{par}"][:, 128:256],
         start=True, stop=True, reads=[A[f"P4{par}"]], writes=[PW], tag=tg + "p8")
    b.ts("V", A[f"P8{par}"][:], PW[:, 0:256], 0.0, None, ALU.add,
         reads=[PW], writes=[A[f"P8{par}"]], tag=tg + "p8e")
    apply_d(3, A[f"P8{par}"][:, 128:256], [A[f"P8{par}"]])
    b.mm(PW[:, 256:384], A[f"P8{par}"][:, 0:128], A[f"P8{par}"][:, 128:256],
         start=True, stop=True, reads=[A[f"P8{par}"]], writes=[PW], tag=tg + "p16")
    b.ts("V", A[f"P16{par}"][:], PW[:, 256:384], 0.0, None, ALU.add,
         reads=[PW], writes=[A[f"P16{par}"]], tag=tg + "p16e")
    apply_d(4, A[f"P16{par}"][:], [A[f"P16{par}"]])
    for args in pend:
        apply_i(*args)
    Z5 = A[f"Za{par4}"]
    for r in range(4):
        row = slice(32 * r, 32 * r + 32)
        b.mm(AUX[row, 320:384], A[f"Kq{par}"][row, 32 * r:32 * r + 32],
             A["Vtm"][row, cg, 64 * h:64 * h + 64], start=True, stop=False,
             tp=(32 * r, 32 * r), reads=[A[f"Kq{par}"], f"Vtm_{cg}_{h}"],
             writes=[AUX], tag=tg + "oi")
        b.mm(AUX[row, 320:384], A[f"Bq{par}"][row, 32 * r:32 * r + 32],
             Z5[row, 64:128], start=False, stop=True, tp=(32 * r, 32 * r),
             reads=[A[f"Bq{par}"], Z5], writes=[AUX], tag=tg + "oi")
    b.copy("V", A[f"OI{par4}"][:], AUX[:, 320:384], reads=[AUX],
           writes=[A[f"OI{par4}"]], tag=tg + "oie")
    for r in range(4):
        row = slice(32 * r, 32 * r + 32)
        b.mm(PW[half:half + 64, 96 * r:96 * r + 64], Z5[row, 0:64],
             A[f"kbv{par4}"][row, 64:128], start=True, stop=False, tp=(32 * r, half),
             reads=[Z5, A[f"kbv{par4}"]], writes=[PW], tag=tg + "eq")
        b.mm(PW[half:half + 64, 96 * r + 64:96 * r + 96], Z5[row, 0:64],
             A[f"Bq{par}"][row, 32 * r:32 * r + 32], start=False, stop=True,
             tp=(32 * r, half), reads=[Z5, A[f"Bq{par}"]], writes=[PW], tag=tg + "eq")
    hs = slice(half, half + 64)
    ev = PW[hs, 0:384].rearrange("p (r q) -> p r q", q=96)
    ist = A["i64st"][hs, :].rearrange("p (u q) -> p u q", u=1).broadcast_to([64, 4, 64])
    EQt = A[f"EQt{eqp}"]
    b.tt("V", EQt[hs, 0:256].rearrange("p (r q) -> p r q", q=64),
         ev[:, :, 0:64], ist, ALU.add, reads=[PW, A["i64st"]],
         writes=[EQt], tag=tg + "ee")
    b.tt("V", EQt[hs, 256:384].rearrange("p (r q) -> p r q", q=32),
         ev[:, :, 64:96],
         A["SC"][hs, cc, 2, 4 * cg:4 * cg + 4, :], ALU.add,
         reads=[PW, A["SC"]], writes=[EQt], tag=tg + "ee")
    lpb64 = A[f"lamp{wpar}"][hs, cc, 4 * cg:4 * cg + 4].rearrange("p (r u) -> p r u", u=1)\
        .broadcast_to([64, 4, 64])
    lpb32 = A[f"lamp{wpar}"][hs, cc, 4 * cg:4 * cg + 4].rearrange("p (r u) -> p r u", u=1)\
        .broadcast_to([64, 4, 32])
    b.tt("V", A[f"EQ{eqp}"][hs, 0:384].rearrange("p (r q) -> p r q", q=96)[:, :, 0:64],
         EQt[hs, 0:256].rearrange("p (r q) -> p r q", q=64), lpb64,
         ALU.mult, reads=[EQt, A[f"lamp{wpar}"]], writes=[A[f"EQ{eqp}"]],
         tag=tg + "ee")
    b.tt("V", A[f"EQ{eqp}"][hs, 0:384].rearrange("p (r q) -> p r q", q=96)[:, :, 64:96],
         EQt[hs, 256:384].rearrange("p (r q) -> p r q", q=32), lpb32,
         ALU.mult, reads=[EQt, A[f"lamp{wpar}"]], writes=[A[f"EQ{eqp}"]],
         tag=tg + "ee")


def emit_seq(b, A, PS, ch, h, NCH, wk, wpar=0):
    cc, half = h // 2, 64 * (h % 2)
    cg, r = ch // 4, ch % 4
    par4 = h % 4
    eqp = (cg * 16 + h) % 4
    hs = slice(half, half + 64)
    row = slice(32 * r, 32 * r + 32)
    tg = f"{wk}s{ch}h{h}"
    EQ = A[f"EQ{eqp}"]
    Z5 = A[f"Za{par4}"]
    Y = A["Ybuf"][hs, h // 2, :]
    import os as _os9
    if _os9.environ.get("OLDREG"):
        yc = 0
        oc = (64, 128, 64, 128)[par4]
        yk = f"pb3y{par4 % 2}"
        qk = f"pb3q{par4 % 2}"
    else:
        yc = 0 if par4 < 2 else 192
        oc = (64, 128, 256, 320)[par4]
        yk = f"pb3y{par4}"
        qk = f"pb3q{par4}"
    YB = PS[3]
    b.mm(YB[hs, yc:yc + 64], EQ[hs, 96 * r:96 * r + 64], Y, start=True, stop=False,
         tp=(half, half), reads=[EQ, f"Y{h}"], writes=[yk], tag=tg + "E")
    b.mm(YB[hs, yc:yc + 64], A[f"kbv{par4}"][row, 0:64],
         A["Vtm"][row, cg, 64 * h:64 * h + 64], start=False, stop=False,
         tp=(32 * r, half), reads=[A[f"kbv{par4}"], f"Vtm_{cg}_{h}"],
         writes=[yk], tag=tg + "z1")
    b.mm(YB[hs, yc:yc + 64], A[f"kbv{par4}"][row, 64:128], Z5[row, 64:128],
         start=False, stop=True, tp=(32 * r, half),
         reads=[A[f"kbv{par4}"], Z5], writes=[yk], tag=tg + "z2")
    b.mm(YB[32 * r:32 * r + 32, oc:oc + 64],
         EQ[hs, 96 * r + 64:96 * r + 96], Y,
         start=True, stop=True, tp=(half, 32 * r), reads=[EQ, f"Y{h}"],
         writes=[qk], tag=tg + "Q")
    if ch < NCH - 1:
        b.ts("V", A["Ybuf"][hs, h // 2, :], YB[hs, yc:yc + 64], 0.0, None, ALU.add,
             reads=[yk], writes=[f"Y{h}"], tag=tg + "Ye")
    else:
        b.ts("V", A["Ybuf"][hs, h // 2, :], YB[hs, yc:yc + 64],
             A[f"lam{wpar}"][hs, cc, NCH - 1:NCH], None, ALU.mult,
             reads=[yk, A[f"lam{wpar}"]], writes=[f"Y{h}"], tag=tg + "Yl")
    b.tt("V", A["Otm"][32 * r:32 * r + 32, cg, 64 * h:64 * h + 64],
         YB[32 * r:32 * r + 32, oc:oc + 64],
         A[f"OI{par4}"][32 * r:32 * r + 32, :], ALU.add,
         reads=[qk, A[f"OI{par4}"]], writes=[f"Otm_{ch}_{h}"], tag=tg + "Oe")


TAPS = {
    "hw": lambda A: (A["hw"][:], [A["hw"]]),
    "rw": lambda A: (A["rw"][:], [A["rw"]]),
    "kw": lambda A: (A["kw"][:], [A["kw"]]),
    "vw": lambda A: (A["vw"][:], [A["vw"]]),
    "aw": lambda A: (A["aw"][:], [A["aw"]]),
    "logw": lambda A: (A["logw"][:], [A["logw"]]),
    "gw": lambda A: (A["gw"][:], [A["gw"]]),
    "k2": lambda A: (A["k2"][:], [A["k2"]]),
    "kkw": lambda A: (A["kkw"][:], [A["kkw"]]),
    "SC": lambda A: (A["SC"][:], [A["SC"]]),
    "ggate": lambda A: (A["ggate"][:], [A["ggate"]]),
    "beta": lambda A: (A["beta"][:], [A["beta"]]),
    "Vtm": lambda A: (A["Vtm"][:], [A["Vtm"]] + [f"Vtm_{cg}_{h}" for cg in range(2) for h in range(16)]),
    "Otm": lambda A: (A["Otm"][:], [A["Otm"]] + [f"Otm_{ch}_{h}" for ch in range(8) for h in range(16)]),
    "goTM": lambda A: (A["goTM"][:], [A["goTM"]]),
    "attw": lambda A: (A["attw"][:], [A["attw"]]),
    "Tq0": lambda A: (A["Tq0"][:], [A["Tq0"]]),
    "Aq0": lambda A: (A["Aq0"][:], [A["Aq0"]]),
    "rhs0": lambda A: (A["rhs0"][:], [A["rhs0"]]),
    "Za0": lambda A: (A["Za0"][:], [A["Za0"]]),
    "kbv0": lambda A: (A["kbv0"][:], [A["kbv0"]]),
    "EQ0": lambda A: (A["EQ0"][0:64, :], [A["EQ0"]]),
    "scal": lambda A: (A["scal"][:], ["scal"]),
    "smv": lambda A: (A["sm"][:], ["sm"]),
    "smgv": lambda A: (A["smg"][:], ["smg"]),
    "lnf": lambda A: (A["lnf"][:], [A["lnf"]]),
    "upk": lambda A: (A["upk"][:], ["upk0", "upk1"]),
    "gateb": lambda A: (A["gateb"][:], [f"gb{ii}" for ii in range(24)]),
    "xbf": lambda A: (A["xbf"][:], [A["xbf"]]),
    "fout": lambda A: (A["fout"][:], [A["fout"]]),
    "wupg": lambda A: (_FFN_DRAM["wupg"], ["wupT_g"]),
    "mxo": lambda A: (_FFN_DRAM["mx_out_d"], ["mx_out"]),
}
_FFN_DRAM = {}


def ffn_phase(b, A, PS, pT, T, n_cores, NSH, HSH, dram, CT, xres_d, up_sp_d,
              wupT_g, wgtT_g, wdnT_g, mx_in, mx_out, scal_d, y_d, collective,
              ln_apply):
    def bcast128(sm_slot, key_in, sc_slot, key_out, tag):
        b.dma("A", scal_d[sc_slot:sc_slot + 1], A["sm"][0:1, sm_slot:sm_slot + 1],
              reads=[key_in, "sm"], writes=["scal_d"], tag=tag)
        b.dma("G", A["scal"][:, sc_slot:sc_slot + 1],
              scal_d[sc_slot:sc_slot + 1][None, :].broadcast_to([128, 1]),
              reads=["scal_d"], writes=[key_out], tag=tag)
    NB = max(T // 256, 1)
    BS = min(256, T)
    NI = 3072
    NTB = max(T // 512, 1)
    TBS = min(512, T)
    NII = NI // 128
    NCC = 8
    xv = xres_d.rearrange("(cc p) t -> p cc t", p=128)
    SM = lambda i: A["sm"][0:1, i:i + 1]
    SC = lambda i: A["scal"][:, i:i + 1]
    M1L, M1G, S1, C1 = 0, 1, 2, 3
    M2L, M3L, M2G, M3G = 4, 5, 6, 7
    S2, C2, S3, C3 = 8, 9, 10, 11
    M4L, M4G, S4, C4 = 12, 13, 14, 15
    M5L, M5G, S42, S5, C5 = 16, 17, 18, 19, 20
    T1, T2 = 30, 31

    b.dma("S", A["id128f"][:], CT["id128f"].ap()[:], writes=[A["id128f"]], tag="fc")
    b.dma("S", A["magt"][:], CT["magt"].ap()[:], writes=[A["magt"]], tag="fc")
    b.memset("V", A["sm"][:], 0.0, writes=["sm"], tag="fz")
    b.memset("V", A["smg"][:], 0.0, writes=["smg"], tag="fz")
    b.memset("V", A["red1"][:], 0.0, writes=["red1"], tag="fz")
    b.dma("A", mx_in[:], A["sm"][0:1, 0:8], reads=["sm"], writes=["mx_in"], tag="fz")

    for blk in range(NB):
        c0 = blk * BS
        b.dma("G", A["xbf"][:, :, 0:BS], xv[:, :, c0:c0 + BS],
              reads=["xres_dram"], writes=[A["xbf"]], tag="ln2l")
        ln_apply(A["xbf"][:, :, 0:BS], BS, A["lnf"][:, :, c0:c0 + BS],
                 A["sqf"][:, :, 0:BS], A["sc1f"][:, :, 0:BS], None,
                 PS[5][:, 0:BS], PS[4][:, 0:BS], tag=f"ln2_{blk}",
                 stat_name="statf")
    b.red("V", A["red1"][:, 0:1], A["lnf"][:, 0:4, :].rearrange("p a t -> p (a t)"), ALU.max, absval=True,
          reads=[A["lnf"]], writes=["red1"], tag="mx1")
    b.red("V", A["red1"][:, 1:2], A["lnf"][:, 4:8, :].rearrange("p a t -> p (a t)"), ALU.max, absval=True,
          reads=[A["lnf"]], writes=["red1"], tag="mx1")
    b.tt("V", A["red1"][:, 2:3], A["red1"][:, 0:1], A["red1"][:, 1:2], ALU.max,
         reads=["red1", "red1"], writes=["red1"], tag="mx1")

    def xpart_max(col_ap, col_keys, dst_slot, dst_key, tag):
        b.mm(PS[6][0:1, 0:128], col_ap, A["id128f"][:], start=True, stop=True,
             reads=list(col_keys) + [A["id128f"]], writes=[PS[6]], tag=tag)
        b.red("V", SM(dst_slot), PS[6][0:1, 0:128], ALU.max, absval=True,
              reads=[PS[6]], writes=[dst_key], tag=tag)

    def agmax(src0, nsrc, wire0, src_keys, out_slots, out_keys, tag):
        b.dma("A", mx_in[wire0:wire0 + nsrc], A["sm"][0:1, src0:src0 + nsrc],
              reads=list(src_keys), writes=["mx_in"], tag=tag)
        collective(mx_out[:], mx_in[:], reads=["mx_in"], writes=["mx_out"])
        b.dma("S", A["smg"][0:1, 0:8 * n_cores],
              mx_out.rearrange("a b -> (a b)")[None, :],
              reads=["mx_out"], writes=["smg"], tag=tag)
        for j, (dst, dkey) in enumerate(zip(out_slots, out_keys)):
            b.red("V", SM(dst),
                  A["smg"][0:1, 0:8 * n_cores]
                  .rearrange("p (r s) -> p r s", s=8)[:, :, wire0 + j],
                  ALU.max, absval=True, reads=["smg"], writes=[dkey], tag=tag)

    xpart_max(A["red1"][:, 2:3], ["red1"], M1L, "sm", "mx1x")
    agmax(M1L, 1, 0, ["sm"], [M1G], ["sm"], "ag1")
    b.ts("V", SM(S1), SM(M1G), 1.0 / QMAX, 1e-12, ALU.mult, ALU.max,
         reads=["sm"], writes=["sm"], tag="sm")
    b.recip(SM(C1), SM(S1), reads=["sm"], writes=["sm"])
    bcast128(C1, "sm", 0, "scal", "s1b")
    for cc in range(NCC):
        f1 = A["fsf1"] if cc % 2 == 0 else A["fsf1b"]
        b.ts("V", f1[:, 0:T], A["lnf"][:, cc, :], SC(0), MAGIC,
             ALU.mult, ALU.add, reads=[A["lnf"], "scal"], writes=[f1],
             tag="xh")
        b.ts("V", A["lnf"][:, cc, :], f1[:, 0:T], MAGIC, None, ALU.subtract,
             reads=[f1], writes=[A["lnf"]], tag="xh")
    for which, wg_d, gkey in [(0, wupT_g, "u"), (1, wgtT_g, "g")]:
        wgk = "wupT_g" if which == 0 else "wgtT_g"
        for ii in range(NII):
            shd, ioff = ii // NSH, (ii % NSH) * 128
            ws = ii % 2
            b.dma("S", A["ffw"][:, ws, :, :],
                  wg_d[shd].rearrange("(cc p) i -> p cc i", p=128)[:, :, ioff:ioff + 128],
                  reads=[wgk], writes=[f"ffw{ws}"], tag="fwl")
            for tb in range(NTB):
                pb = PS[tb % 4]
                for cc in range(NCC):
                    b.mm(pb[:, 0:TBS], A["ffw"][:, ws, cc, :],
                         A["lnf"][:, cc, TBS * tb:TBS * tb + TBS],
                         start=(cc == 0), stop=(cc == NCC - 1),
                         reads=[f"ffw{ws}", A["lnf"]], writes=[pb], tag="fmm")
                dst = (A["gateb"][:, ii, TBS * tb:TBS * tb + TBS] if which
                       else A["upk"][:, ii % 2, TBS * tb:TBS * tb + TBS])
                b.ts("V", dst, pb[:, 0:TBS], 0.0, None, ALU.add,
                     reads=[pb], writes=[f"{gkey}b{ii}"] + ([] if which else ["upk" + str(ii % 2)]), tag="fme")
            src_buf = A["gateb"][:, ii, :] if which else A["upk"][:, ii % 2, :]
            if which == 0:
                b.dma("A", up_sp_d.rearrange("(jj p) t -> p jj t", p=128)[:, ii, :],
                      A["upk"][:, ii % 2, :], reads=[f"ub{ii}", "upk" + str(ii % 2)], writes=["up_sp"],
                      tag="fsp")
            b.red("V", A["red1"][:, 3:4],
                  src_buf, ALU.max, absval=True, reads=[f"{gkey}b{ii}"],
                  writes=["red1"], tag="fmx")
            if ii == 0:
                b.copy("V", A["red1"][:, 5 + which:6 + which], A["red1"][:, 3:4],
                       reads=["red1"], writes=[f"racc{which}"], tag="fmx")
            else:
                b.tt("V", A["red1"][:, 5 + which:6 + which],
                     A["red1"][:, 5 + which:6 + which], A["red1"][:, 3:4], ALU.max,
                     reads=[f"racc{which}", "red1"], writes=[f"racc{which}"],
                     tag="fmx")
    xpart_max(A["red1"][:, 5:6], ["red1"], M2L, "sm", "mx2x")
    xpart_max(A["red1"][:, 6:7], ["red1"], M3L, "sm", "mx3x")
    agmax(M2L, 2, 1, ["sm", "sm"], [M2G, M3G], ["sm", "sm"], "ag2")
    for mg, mk, s_, c_, sk, ck in [(M2G, "sm", S2, C2, "sm", "sm"),
                                   (M3G, "sm", S3, C3, "sm", "sm")]:
        b.ts("V", SM(T1), SM(mg), SM(S1), 1.0 / QMAX, ALU.mult, ALU.mult,
             reads=[mk, "sm"], writes=["sm"], tag=sk)
        b.ts("V", SM(s_), SM(T1), 1e-12, None, ALU.max,
             reads=["sm"], writes=[sk], tag=sk)
        b.recip(SM(T2), SM(s_), reads=[sk], writes=["sm"])
        b.ts("V", SM(c_), SM(T2), SM(S1), None, ALU.mult,
             reads=["sm", "sm"], writes=[ck], tag=ck)
    bcast128(C2, "sm", 1, "scal", "cb")
    bcast128(C3, "sm", 2, "scal", "cb")
    bcast128(S3, "sm", 3, "scal", "cb")
    for ii in range(NII):
        gb = A["gateb"][:, ii, :]
        f1 = A["fsf1"][:, 0:T] if ii % 2 == 0 else A["fsf1b"][:, 0:T]
        f1k = A["fsf1"] if ii % 2 == 0 else A["fsf1b"]
        u0 = A["upk"][:, 0, 0:T] if ii % 2 == 0 else A["upkb"][:, 0:T]
        u0k = "upk0" if ii % 2 == 0 else A["upkb"]
        b.act(f1, gb, AF.Identity, bias=A["magt"][:, 0:1],
              scale=SC(2), reads=[f"gb{ii}", "scal", A["magt"]],
              writes=[f1k], tag="g3a")
        b.ts("V", u0, f1, MAGIC, None, ALU.subtract,
             reads=[f1k], writes=[u0k], tag="g3b")
        b.act(A["fsf2"][:, 0:T], u0, AF.Sigmoid, scale=SC(3),
              reads=[u0k, "scal"], writes=[A["fsf2"]], tag="g3c")
        b.act(f1, A["fsf2"][:, 0:T], AF.Identity,
              bias=A["magt"][:, 0:1], scale=65536.0,
              reads=[A["fsf2"], A["magt"]], writes=[f1k], tag="g3d")
        b.ts("V", A["fsf2"][:, 0:T], f1, MAGIC, 65535.0,
             ALU.subtract, ALU.min, reads=[f1k], writes=[A["fsf2"]],
             tag="g3e")
        b.tt("G", gb, u0, A["fsf2"][:, 0:T], ALU.mult,
             reads=[u0k, A["fsf2"]], writes=[f"gb{ii}"], tag="g3f")
        b.red("V", A["red1"][:, 3:4], gb, ALU.max, absval=True,
              reads=[f"gb{ii}"], writes=["red1"], tag="g3g")
        if ii == 0:
            b.copy("V", A["red1"][:, 7:8], A["red1"][:, 3:4], reads=["red1"],
                   writes=["red1"], tag="g3h")
        else:
            b.tt("V", A["red1"][:, 7:8], A["red1"][:, 7:8], A["red1"][:, 3:4],
                 ALU.max, reads=["red1", "red1"], writes=["red1"], tag="g3h")
    xpart_max(A["red1"][:, 7:8], ["red1"], M4L, "sm", "mx4x")
    agmax(M4L, 1, 3, ["sm"], [M4G], ["sm"], "ag3")
    b.ts("V", SM(T1), SM(M4G), SM(S3), 1.0 / 65536.0 / QMAX, ALU.mult, ALU.mult,
         reads=["sm", "sm"], writes=["sm"], tag="sm")
    b.ts("V", SM(S4), SM(T1), 1e-12, None, ALU.max, reads=["sm"], writes=["sm"],
         tag="sm")
    b.recip(SM(T2), SM(S4), reads=["sm"], writes=["sm"])
    b.ts("V", SM(C4), SM(T2), SM(S3), 1.0 / 65536.0, ALU.mult, ALU.mult,
         reads=["sm", "sm"], writes=["sm"], tag="sm")
    bcast128(C4, "sm", 4, "scal", "cb")
    for ii in range(NII):
        gb = A["gateb"][:, ii, :]
        f1 = A["fsf1"][:, 0:T] if ii % 2 == 0 else A["fsf1b"][:, 0:T]
        f1k = A["fsf1"] if ii % 2 == 0 else A["fsf1b"]
        u0 = A["upk"][:, 0, 0:T] if ii % 2 == 0 else A["upkb"][:, 0:T]
        u0k = "upk0" if ii % 2 == 0 else A["upkb"]
        b.dma("S", A["upk"][:, 1, 0:T],
              up_sp_d.rearrange("(jj p) t -> p jj t", p=128)[:, ii, :],
              reads=["up_sp"], writes=["upk1"], tag="g4l")
        b.act(f1, gb, AF.Identity, bias=A["magt"][:, 0:1],
              scale=SC(4), reads=[f"gb{ii}", "scal", A["magt"]],
              writes=[f1k], tag="g4a")
        b.ts("V", u0, f1, MAGIC, None, ALU.subtract,
             reads=[f1k], writes=[u0k], tag="g4b")
        b.act(f1, A["upk"][:, 1, 0:T], AF.Identity,
              bias=A["magt"][:, 0:1], scale=SC(1),
              reads=["upk1", "scal", A["magt"]], writes=[f1k], tag="g4c")
        b.ts("V", A["fsf2"][:, 0:T], f1, MAGIC, None, ALU.subtract,
             reads=[f1k], writes=[A["fsf2"]], tag="g4d")
        b.tt("V", gb, u0, A["fsf2"][:, 0:T], ALU.mult,
             reads=[u0k, A["fsf2"]], writes=[f"gb{ii}"], tag="g4e")
        b.red("V", A["red1"][:, 3:4], gb, ALU.max, absval=True,
              reads=[f"gb{ii}"], writes=["red1"], tag="g4f")
        if ii == 0:
            b.copy("V", A["red1"][:, 7:8], A["red1"][:, 3:4], reads=["red1"],
                   writes=["red1"], tag="g4g")
        else:
            b.tt("V", A["red1"][:, 7:8], A["red1"][:, 7:8], A["red1"][:, 3:4],
                 ALU.max, reads=["red1", "red1"], writes=["red1"], tag="g4g")
    xpart_max(A["red1"][:, 7:8], ["red1"], M5L, "sm", "mx5x")
    agmax(M5L, 1, 4, ["sm"], [M5G], ["sm"], "ag4")
    b.ts("V", SM(S42), SM(S4), SM(S2), None, ALU.mult, reads=["sm", "sm"],
         writes=["sm"], tag="sm")
    b.ts("V", SM(T1), SM(M5G), SM(S42), 1.0 / QMAX, ALU.mult, ALU.mult,
         reads=["sm", "sm"], writes=["sm"], tag="sm")
    b.ts("V", SM(S5), SM(T1), 1e-12, None, ALU.max, reads=["sm"], writes=["sm"],
         tag="sm")
    b.recip(SM(T2), SM(S5), reads=["sm"], writes=["sm"])
    b.ts("V", SM(C5), SM(T2), SM(S42), None, ALU.mult, reads=["sm", "sm"],
         writes=["sm"], tag="sm")
    bcast128(C5, "sm", 5, "scal", "cb")
    bcast128(S5, "sm", 6, "scal", "cb")
    for ii in range(NII):
        gb = A["gateb"][:, ii, :]
        f1 = A["fsf1"][:, 0:T] if ii % 2 == 0 else A["fsf1b"][:, 0:T]
        f1k = A["fsf1"] if ii % 2 == 0 else A["fsf1b"]
        b.act(f1, gb, AF.Identity, bias=A["magt"][:, 0:1],
              scale=SC(5), reads=[f"gb{ii}", "scal", A["magt"]],
              writes=[f1k], tag="g5a")
        b.ts("V", gb, f1, MAGIC, None,
             ALU.subtract, reads=[f1k], writes=[f"gb{ii}"], tag="g5b")
    b.barrier()
    TB2 = min(512, T)
    NTB2 = max(T // TB2, 1)
    for hh in range(NCC):
        ws = hh % 2
        b.dma("S", A["dnw"][:, ws, :, :],
              wdnT_g[hh].rearrange("(jj p) h -> p jj h", p=128),
              reads=["wdnT_g"], writes=[f"dnw{ws}"], tag="dwl")
        for ii in range(NII):
            for tb in range(NTB2):
                pb = PS[tb % 4]
                b.mm(pb[:, 0:TB2], A["dnw"][:, ws, ii, :],
                     A["gateb"][:, ii, tb * TB2:(tb + 1) * TB2],
                     start=(ii == 0), stop=(ii == NII - 1),
                     reads=[f"dnw{ws}", f"gb{ii}"], writes=[pb], tag="dmm")
        for tb in range(NTB2):
            b.ts("V", A["lnf"][:, hh, tb * TB2:(tb + 1) * TB2],
                 PS[tb % 4][:, 0:TB2], SC(6), None, ALU.mult,
                 reads=[PS[tb % 4], "scal"], writes=[A["lnf"]], tag="dme")
    for blk in range(NB):
        c0 = blk * BS
        b.dma("S", A["fres"][:, :, 0:BS], xv[:, :, c0:c0 + BS],
              reads=["xres_dram"], writes=[A["fres"]], tag="fl")
        ln_apply(A["lnf"][:, :, c0:c0 + BS], BS, A["fout"][:, :, 0:BS],
                 A["sqf"][:, :, 0:BS], A["sc1f"][:, :, 0:BS],
                 A["fsf1"][:, 0:NCC * BS].rearrange("p (cc t) -> p cc t", t=BS),
                 PS[5][:, 0:BS], PS[4][:, 0:BS], extra=A["fres"][:, :, 0:BS],
                 tag=f"fln{blk}", stat_name="statf")
        b.dma("A", y_d.rearrange("(cc p) t -> p cc t", p=128)[:, :, c0:c0 + BS],
              A["fout"][:, :, 0:BS], reads=[A["fout"]], writes=["y_out"],
              tag="yo")


from concourse.bass_utils import run_bass_kernel_spmd

_NC_CACHE = {}

def _build():
    if "nc" not in _NC_CACHE:
        _NC_CACHE["nc"] = build_program(T=2048, W=128, n_cores=8, taps=(),
                                        upto="full", ffn_prep=True)[0]
    return _NC_CACHE["nc"]


def kernel(**inputs):
    P = {k[len("L0_attn_"):]: np.asarray(v) for k, v in inputs.items()
         if k.startswith("L0_attn_")}
    F = {k[len("L0_ffn_"):]: np.asarray(v) for k, v in inputs.items()
         if k.startswith("L0_ffn_")}
    x = np.ascontiguousarray(P.pop("hidden_states").astype(np.float32, copy=False))
    n_cores = 8
    f32 = np.float32
    vecs = np.ascontiguousarray(np.stack(
        [P["x_r"], P["x_w"], P["x_k"], P["x_v"], P["x_a"], P["k_k"],
         P["k_a"], P["r_k"].reshape(-1), P["w_lora_bias"],
         P["a_lora_bias"], P["g_lora_bias"], P["gn_weight"],
         P["gn_bias"]]).astype(f32).T)
    vrows = np.ascontiguousarray(vecs.T)
    sh = 3072 // n_cores
    shared = {
        "W_r": np.ascontiguousarray(P["W_r"], f32),
        "W_k": np.ascontiguousarray(P["W_k"], f32),
        "W_v": np.ascontiguousarray(P["W_v"], f32),
        "W_o": np.ascontiguousarray(P["W_o"], f32),
        "w_lora_a": np.ascontiguousarray(P["w_lora_a"], f32),
        "w_lora_b": np.ascontiguousarray(P["w_lora_b"], f32),
        "a_lora_a": np.ascontiguousarray(P["a_lora_a"], f32),
        "a_lora_b": np.ascontiguousarray(P["a_lora_b"], f32),
        "g_lora_a": np.ascontiguousarray(P["g_lora_a"], f32),
        "g_lora_b": np.ascontiguousarray(P["g_lora_b"], f32),
        "vecs": vecs, "vrows": vrows,
    }
    wu = np.ascontiguousarray(F["w_up"], f32)
    wg = np.ascontiguousarray(F["w_gate"], f32)
    wd = np.ascontiguousarray(F["w_down"], f32)
    in_maps = []
    for c in range(n_cores):
        m = dict(shared)
        m["xT"] = np.ascontiguousarray(x[c].T)
        m["ffn_up_sh"] = np.ascontiguousarray(wu[c * sh:(c + 1) * sh])
        m["ffn_gate_sh"] = np.ascontiguousarray(wg[c * sh:(c + 1) * sh])
        m["ffn_down_sh"] = np.ascontiguousarray(wd[c * 128:(c + 1) * 128])
        in_maps.append(m)
    nc = _build()
    res = run_bass_kernel_spmd(nc, in_maps, core_ids=list(range(n_cores)))
    out = np.stack([np.ascontiguousarray(r["y"].T) for r in res.results])
    return out.astype(np.float32, copy=False)


# version 17 (final)
# speedup vs baseline: 1.0018x; 1.0018x over previous
import numpy as np
import ml_dtypes
import concourse.bass as bass
import concourse.mybir as mybir

F32 = mybir.dt.float32
BF16 = mybir.dt.bfloat16
AF = mybir.ActivationFunctionType
ALU = mybir.AluOpType
AX = mybir.AxisListType

MAGIC = float(np.float32(1.5 * 2.0**23))
C, H, D, L, NCC = 1024, 16, 64, 32, 8
QMAX = 32767.0
EPS = 1e-5

ENGS = ("T", "V", "A", "G", "S")
DMAQ = {"S": "Sq", "A": "Aq", "G": "Gq"}
NDMASEM = 8


class KB:
    def __init__(self, nc):
        self.nc = nc
        self.prog = {e: [] for e in ENGS}
        self.cnt = {e: 0 for e in ENGS}
        self.dcnt = {f"{q}{j}": 0 for q in DMAQ.values() for j in range(NDMASEM)}
        self.didx = {q: 0 for q in DMAQ.values()}
        self.last_w = {}
        self.all_w = {}
        self.snaps = {}
        self.readers = {}
        self.waited = {e: {} for e in ENGS}
        self.sems = {}

    @staticmethod
    def key(x):
        if isinstance(x, str):
            return x
        t = getattr(x, "tensor", x)
        return t.name

    def _want(self, eng, tok):
        sem, val = tok
        w = self.waited[eng]
        if w.get(sem, 0) >= val:
            return None
        w[sem] = val
        snap = self.snaps.get(tok)
        if snap:
            for s2, v2 in snap.items():
                if w.get(s2, 0) < v2:
                    w[s2] = v2
        return tok

    def op(self, eng, fn, reads=(), writes=(), dma=False, tag="", skip_waw=False):
        rkeys = [self.key(x) for x in reads]
        wkeys = [self.key(x) for x in writes]
        waits = []
        for k in rkeys:
            for t in self.all_w.get(k, {}).items():
                t2 = self._want(eng, t)
                if t2:
                    waits.append(t2)
        for k in wkeys:
            if not skip_waw:
                for t in self.all_w.get(k, {}).items():
                    t2 = self._want(eng, t)
                    if t2:
                        waits.append(t2)
            for t in self.readers.get(k, ()):
                t2 = self._want(eng, t)
                if t2:
                    waits.append(t2)
        if dma:
            q = DMAQ[eng]
            j = self.didx[q] % NDMASEM
            self.didx[q] += 1
            qs = f"{q}{j}"
            prev = self.dcnt[qs]
            if prev > 0:
                t2 = self._want(eng, (qs, prev))
                if t2:
                    waits.append(t2)
            self.dcnt[qs] = prev + 16
            tok = (qs, self.dcnt[qs])
        else:
            self.cnt[eng] += 1
            tok = (eng, self.cnt[eng])
        snap = dict(self.waited[eng])
        snap[eng] = self.cnt[eng]
        if dma:
            snap[tok[0]] = tok[1]
        self.snaps[tok] = snap
        for k in rkeys:
            self.readers.setdefault(k, []).append(tok)
        for k in wkeys:
            self.last_w[k] = tok
            d = self.all_w.setdefault(k, {})
            d[tok[0]] = max(d.get(tok[0], 0), tok[1])
            self.readers[k] = []
        self.prog[eng].append((waits, fn, tok, tag))
        return tok

    def barrier(self):
        toks = [(e, self.cnt[e]) for e in ENGS if self.cnt[e] > 0]
        toks += [(qs, v) for qs, v in self.dcnt.items() if v > 0]
        for e in ENGS:
            waits = []
            for t in toks:
                t2 = self._want(e, t)
                if t2:
                    waits.append(t2)
            self.cnt[e] += 1
            self.prog[e].append((waits, None, (e, self.cnt[e]), "barrier"))

    def realize(self, block):
        nc = self.nc
        for name in list(ENGS) + list(self.dcnt):
            self.sems[name] = nc.alloc_semaphore("sem_" + name)
        deco = {"T": block.tensor, "V": block.vector, "A": block.scalar,
                "G": block.gpsimd, "S": block.sync}

        def run_stream(eng, ename):
            for waits, fn, tok, tag in self.prog[ename]:
                for s, v in waits:
                    eng.wait_ge(self.sems[s], v)
                sem, _ = tok
                ins = eng.nop() if fn is None else fn(eng)
                ins.then_inc(self.sems[sem], 16 if sem in self.dcnt else 1)

        for ename in ENGS:
            if self.prog[ename]:
                deco[ename](lambda eng, en=ename: run_stream(eng, en))

    def dma(self, eng, out, in_, reads=(), writes=(), tag=""):
        return self.op(eng, lambda e: e.dma_start(out=out, in_=in_),
                       reads=reads, writes=writes, dma=True, tag=tag)

    @staticmethod
    def _free_dims(ap_obj):
        try:
            pairs = [(int(p[0]), int(p[1])) for p in ap_obj.ap]
        except Exception:
            return 1
        free = [(st, sz) for st, sz in pairs[1:] if sz > 1]
        if not free:
            return 1
        n, cst, csz = 1, free[-1][0], free[-1][1]
        for st, sz in reversed(free[:-1]):
            if st == cst * csz:
                csz *= sz
            else:
                n += 1
                cst, csz = st, sz
        return n

    def _ck(self, ap_obj, what, tag):
        if self._free_dims(ap_obj) > 1:
            raise RuntimeError(f"matmul {what} has multi-dim free AP at {tag}: "
                               f"{[list(p) for p in ap_obj.ap]}")

    def mm(self, out, lhsT, rhs, start=True, stop=True, reads=(), writes=(),
           tp=None, tag=""):
        self._ck(lhsT, "lhsT", tag); self._ck(rhs, "rhs", tag)
        return self.op("T", lambda e: e.matmul(out, lhsT, rhs, start=start,
                                               stop=stop, tile_position=tp),
                       reads=reads, writes=writes, tag=tag,
                       skip_waw=not start)

    def trans(self, out, in_, ident, reads=(), writes=(), tp=None, tag=""):
        self._ck(in_, "trans_in", tag)
        return self.op("T", lambda e: e.transpose(out, in_, ident,
                                                  tile_position=tp),
                       reads=reads, writes=writes, tag=tag)

    def tt(self, eng, out, in0, in1, alu, reads=(), writes=(), tag=""):
        return self.op(eng, lambda e: e.tensor_tensor(out=out, in0=in0,
                                                      in1=in1, op=alu),
                       reads=reads, writes=writes, tag=tag)

    def ts(self, eng, out, in0, s1, s2, op0, op1=None, reads=(), writes=(), tag=""):
        if s2 is None:
            return self.op(eng, lambda e: e.tensor_single_scalar(
                out=out, in_=in0, scalar=s1, op=op0),
                reads=reads, writes=writes, tag=tag)
        return self.op(eng, lambda e: e.tensor_scalar(
            out=out, in0=in0, scalar1=s1, scalar2=s2, op0=op0, op1=op1),
            reads=reads, writes=writes, tag=tag)

    def stt(self, eng, out, in0, scalar, in1, op0, op1, reads=(), writes=(), tag=""):
        return self.op(eng, lambda e: e.scalar_tensor_tensor(
            out=out, in0=in0, scalar=scalar, in1=in1, op0=op0, op1=op1),
            reads=reads, writes=writes, tag=tag)

    def act(self, out, in_, func, bias=0.0, scale=1.0, reads=(), writes=(), tag=""):
        return self.op("A", lambda e: e.activation(out=out, in_=in_, func=func,
                                                   bias=bias, scale=scale),
                       reads=reads, writes=writes, tag=tag)

    def red(self, eng, out, in_, alu, absval=False, reads=(), writes=(), tag=""):
        return self.op(eng, lambda e: e.tensor_reduce(
            out=out, in_=in_, axis=AX.X, op=alu, apply_absolute_value=absval),
            reads=reads, writes=writes, tag=tag)

    def copy(self, eng, out, in_, reads=(), writes=(), tag=""):
        if eng == "A":
            return self.op(eng, lambda e: e.copy(out, in_),
                           reads=reads, writes=writes, tag=tag)
        return self.op(eng, lambda e: e.tensor_copy(out=out, in_=in_),
                       reads=reads, writes=writes, tag=tag)

    def scan_op(self, eng, out, d0, d1, init, op0, op1, reads=(), writes=(), tag=""):
        return self.op(eng, lambda e: e.tensor_tensor_scan(
            out=out, data0=d0, data1=d1, initial=init, op0=op0, op1=op1),
            reads=reads, writes=writes, tag=tag)

    def memset(self, eng, ap, val, writes=(), tag=""):
        return self.op(eng, lambda e: e.memset(ap, val), writes=writes, tag=tag)

    def recip(self, out, in_, reads=(), writes=(), tag=""):
        return self.op("V", lambda e: e.reciprocal(out=out, in_=in_),
                       reads=reads, writes=writes, tag=tag)


VR = dict(x_r=0, x_w=1, x_k=2, x_v=3, x_a=4, k_k=5, k_a=6, r_k=7,
          w_lora_bias=8, a_lora_bias=9, g_lora_bias=10, gn_weight=11, gn_bias=12)


def build_program(T=2048, W=256, n_cores=8, taps=(), upto="full", ffn_prep=True, only_head=None):
    nc = bass.Bass("TRN2", target_bir_lowering=False)
    NW, NCH = T // W, W // L
    NCG, NT = NCH // 4, W // 128
    W1 = W + 1
    NI = 3072
    NSH = NI // n_cores // 128
    HSH = C // n_cores

    dram = {}
    def din(name, shape, dt=F32):
        dram[name] = nc.dram_tensor(name, shape, dt, kind="ExternalInput").ap()
        return dram[name]
    xT_d = din("xT", [C, T])
    for w in ("W_r", "W_k", "W_v", "W_o"):
        din(w, [C, C])
    din("w_lora_a", [C, 64]); din("w_lora_b", [64, C])
    din("a_lora_a", [C, 64]); din("a_lora_b", [64, C])
    din("g_lora_a", [C, 128]); din("g_lora_b", [128, C])
    din("vecs", [C, 13]); din("vrows", [13, C])
    din("ffn_up_sh", [NI // n_cores, C])
    din("ffn_gate_sh", [NI // n_cores, C])
    din("ffn_down_sh", [HSH, NI])
    y_d = nc.dram_tensor("y", [C, T], F32, kind="ExternalOutput").ap()
    tap_d = {}
    for tname, tshape, tdt in taps:
        tap_d[tname] = nc.dram_tensor("tap_" + tname, tshape, tdt,
                                      kind="ExternalOutput").ap()

    xres_d = nc.dram_tensor("xres_d", [C, T], F32, kind="Internal").ap()
    up_sp_d = nc.dram_tensor("up_spill", [NI, T], BF16, kind="Internal").ap()
    wupT_sh = nc.dram_tensor("wupT_sh", [C, NI // n_cores], BF16, kind="Internal").ap()
    wgtT_sh = nc.dram_tensor("wgtT_sh", [C, NI // n_cores], BF16, kind="Internal").ap()
    wdnT_sh = nc.dram_tensor("wdnT_sh", [NI, HSH], BF16, kind="Internal").ap()
    wupT_g = nc.dram_tensor("wupT_g", [n_cores, C, NI // n_cores], BF16, kind="Internal").ap()
    wgtT_g = nc.dram_tensor("wgtT_g", [n_cores, C, NI // n_cores], BF16, kind="Internal").ap()
    wdnT_g = nc.dram_tensor("wdnT_g", [n_cores, NI, HSH], BF16, kind="Internal").ap()
    mx_in = nc.dram_tensor("mx_in", [8], F32, kind="Internal").ap()
    scal_d = nc.dram_tensor("scal_d", [8], F32, kind="Internal").ap()
    mx_out = nc.dram_tensor("mx_out", [n_cores, 8], F32, kind="Internal").ap()
    _FFN_DRAM.clear()
    _FFN_DRAM.update(wupg=wupT_g, mx_out_d=mx_out, xres_dd=xres_d)

    bfd = ml_dtypes.bfloat16
    eye = lambda n: np.eye(n, dtype=np.float32)
    rp = np.ones((128, W), np.float32); rp[:, ::L] = 0.0
    CT = {}
    CT["rp"] = nc.inline_tensor(rp, "c_rp")
    CT["id64d"] = nc.inline_tensor(np.vstack([eye(64), eye(64)]).astype(bfd), "c_id64d")
    CT["id16"] = nc.inline_tensor(eye(16).astype(bfd), "c_id16")
    CT["id128"] = nc.inline_tensor(eye(128).astype(bfd), "c_id128")
    CT["id128f"] = nc.inline_tensor(eye(128), "c_id128f")
    CT["magt"] = nc.inline_tensor(np.full((128, 1), 1.5 * 2.0**23, np.float32), "c_magt")
    CT["i64st"] = nc.inline_tensor(np.vstack([eye(64), eye(64)]).astype(np.float32), "c_i64st")
    def quadmask(k):
        m = np.zeros((128, 128), np.float32)
        for j in range(4):
            blk = (np.tril(np.ones((32, 32)), -1) if k == -1 else
                   np.triu(np.ones((32, 32)), 1) if k == 1 else
                   np.triu(np.ones((32, 32)), 0))
            m[32 * j:32 * j + 32, 32 * j:32 * j + 32] = blk
        return m.astype(np.float32)
    CT["mtril"] = nc.inline_tensor(quadmask(-1).astype(bfd), "c_mtril")
    CT["mtriu"] = nc.inline_tensor(quadmask(+1).astype(bfd), "c_mtriu")
    CT["mtriu0"] = nc.inline_tensor(quadmask(0).astype(bfd), "c_mtriu0")
    CT["ones"] = nc.inline_tensor(np.ones((128, 1), bfd), "c_ones")
    CT["cst"] = nc.inline_tensor(np.tile(np.array([[0.0, 1.0, EPS]], np.float32), (128, 1)), "c_cst")
    CT["onesr"] = nc.inline_tensor(np.ones((1, 128), np.float32), "c_onesr")
    for cc in range(NCC):
        m = np.zeros((128, 16), np.float32)
        m[0:64, 2 * cc] = 1.0; m[64:128, 2 * cc + 1] = 1.0
        CT[f"hsum{cc}"] = nc.inline_tensor(m.astype(bfd), f"c_hsum{cc}")
        bm = np.zeros((16, 128), np.float32)
        bm[2 * cc, 0:64] = 1.0; bm[2 * cc + 1, 64:128] = 1.0
        CT[f"hbc{cc}"] = nc.inline_tensor(bm.astype(bfd), f"c_hbc{cc}")

    b = KB(nc)
    A = {}
    regions = {"D": None, "F": None}
    rcur = {}
    import math as _math
    def _dtb(dt):
        return 2 if dt == BF16 else 4
    def sb(name, shape, dt, region=None):
        if region is None:
            A[name] = nc.alloc_sbuf_tensor("s_" + name, shape, dt)
            return A[name]
        nbytes = int(np.prod(shape[1:])) * _dtb(dt)
        off = rcur[region]
        off = (off + 31) // 32 * 32
        rcur[region] = off + nbytes
        assert rcur[region] <= nc.SBUF_PARTITION_SIZE_BYTES, \
            (name, region, rcur[region])
        A[name] = nc.alloc_sbuf_tensor_at("s_" + name, shape, dt, offset=off)
        return A[name]

    fbase = nc.SBUF_PARTITION_SIZE_BYTES - nc.sbuf_bytes_remaining
    sb("Wr", [128, NCC, C], BF16); sb("Wk", [128, NCC, C], BF16)
    sb("Wv", [128, NCC, C], BF16); sb("Wo", [128, NCC, C], BF16)
    sb("wla", [128, NCC, 64], BF16); sb("wlb", [64, C], BF16)
    sb("ala", [128, NCC, 64], BF16); sb("alb", [64, C], BF16)
    sb("gla", [128, NCC, 128], BF16); sb("glb", [128, C], BF16)
    sb("vecs", [128, NCC, 13], F32)
    sb("vaux", [128, 2, NCC], F32)
    sb("gnw_t", [128, C], BF16); sb("gnb_t", [128, C], BF16)
    sb("gbias_t", [128, C], BF16)
    sb("id64d", [128, 64], BF16); sb("id16", [16, 16], BF16)
    sb("id128", [128, 128], BF16)
    sb("i64st", [128, 64], F32)
    sb("mtril", [128, 128], BF16); sb("mtriu", [128, 128], BF16)
    sb("mtriu0", [128, 128], BF16)
    sb("rp", [128, W], F32)
    for cc in range(NCC):
        sb(f"hsum{cc}", [128, 16], BF16); sb(f"hbc{cc}", [16, 128], BF16)
    wbase = nc.SBUF_PARTITION_SIZE_BYTES - nc.sbuf_bytes_remaining
    sb("xb0", [128, NCC, W1], BF16); sb("xb1", [128, NCC, W1], BF16)
    sb("xf0", [128, NCC, W], F32); sb("xf1", [128, NCC, W], F32)
    sb("xrnew", [128, NCC, W], F32)
    sb("hw", [128, NCC, W1], BF16)
    sb("dl", [128, NCC, W], BF16)
    sb("mixA", [128, NCC, W], BF16)
    sb("mixB", [128, NCC, W], BF16)
    sb("lnsA", [128, NCC, W1], BF16)
    sb("sq", [128, NCC, W1], BF16)
    sb("stat", [1, 5, 256], F32)
    sb("rw", [128, NCC, W], BF16)
    sb("kw", [128, NCC, W], BF16)
    sb("vw", [128, NCC, W], BF16)
    sb("aw", [128, NCC, W], BF16)
    sb("f32a", [128, NCC, W], F32)
    sb("f32b", [128, NCC, W], F32)
    sb("logw", [128, NCC, W], F32)
    sb("gw", [128, NCC, W], F32)
    sb("eg", [128, NCC, W], BF16)
    sb("emg", [128, NCC, W], BF16)
    sb("egp", [128, NCC, W], BF16)
    sb("lam0", [128, NCC, NCH], F32); sb("lam1", [128, NCC, NCH], F32)
    sb("lamp0", [128, NCC, NCH], F32); sb("lamp1", [128, NCC, NCH], F32)
    sb("k2", [128, NCC, W], BF16)
    sb("kkw", [128, NCC, W], BF16)
    sb("braw", [128, NCC, W], BF16)
    sb("rkp", [128, NCC, W], BF16)
    sb("m1", [128, NCC, W], BF16)
    sb("kn16", [16, W], F32); sb("kn16b", [16, W], F32)
    sb("kinvb", [16, W], BF16)
    sb("beta", [16, W], BF16)
    sb("betat0", [128, NT, 16], BF16); sb("betat1", [128, NT, 16], BF16)
    sb("th64", [64, W], BF16)
    sb("sg", [128, W], BF16)
    sb("ggate0", [128, NT, C], BF16); sb("ggate1", [128, NT, C], BF16)
    sb("SC", [128, NCC, 4, NCH, L], BF16)
    sb("Vtm", [128, NT, C], BF16)
    sb("Otm", [128, NT, C], F32)
    for par in range(2):
        sb(f"Tq{par}", [128, 128], BF16); sb(f"Aq{par}", [128, 128], BF16)
        sb(f"Mq{par}", [128, 128], BF16); sb(f"Kq{par}", [128, 128], BF16)
        sb(f"Bq{par}", [128, 128], BF16)
        sb(f"P2{par}", [128, 256], BF16); sb(f"P4{par}", [128, 256], BF16)
        sb(f"P8{par}", [128, 256], BF16); sb(f"P16{par}", [128, 128], BF16)
        sb(f"rhs{par}", [128, 128], BF16)
        sb(f"Zb{par}", [128, 128], BF16)
        sb(f"Zc{par}", [128, 128], BF16)
    for par in range(4):
        sb(f"kbv{par}", [128, 192], BF16)
        sb(f"Za{par}", [128, 128], BF16)
        sb(f"OI{par}", [128, 64], F32)
    for eqp in range(4):
        sb(f"EQt{eqp}", [128, 384], BF16)
        sb(f"EQ{eqp}", [128, 384], BF16)
    sb("Ybuf", [128, NCC, 64], BF16)
    sb("Ysc", [128, 64], F32)
    sb("goTM", [128, NT, C], BF16)
    sb("goFM", [128, NCC, W], BF16)
    sb("attw", [128, NCC, W], BF16)
    sb("gn16", [128, NT, 16], F32); sb("gn16b", [128, NT, 16], F32)
    sb("gnA", [128, NT, 16], F32); sb("gnB", [128, NT, 16], F32)
    sb("scal", [128, 8], F32)
    sb("sm", [1, 64], F32)
    sb("red1", [128, 8], F32)
    sb("ones", [128, 1], BF16); sb("onesr", [1, 128], F32)
    sb("cst", [128, 3], F32)
    rcur["D"] = wbase
    sb("fsc1", [128, 2048], F32, region="D")
    sb("fsc2", [128, 2048], F32, region="D")
    sb("fsh", [128, max(NSH, 1), C], F32, region="D")
    sb("fshT", [128, NCC, NSH * 128], BF16, region="D")
    sb("dqs", [128, 96], F32, region="D"); sb("dqr", [128, 96], F32, region="D")
    sb("dsh", [128, NI], F32, region="D")
    sb("dshT", [128, NI // 128, HSH], BF16, region="D")
    sb("wqtmp", [128, C], BF16, region="D")
    sb("dq3", [128, NI], BF16, region="D")

    rcur["F"] = fbase
    sb("ffw", [128, 2, NCC, 128], BF16, region="F")
    sb("gateb", [128, 24, T], BF16, region="F")
    sb("lnf", [128, NCC, T], BF16, region="F")
    sb("fout", [128, NCC, 256], F32, region="F")
    sb("fres", [128, NCC, 256], F32, region="F")
    sb("fsf1", [128, max(T, 2048)], F32, region="F")
    sb("xbf", [128, NCC, 256], BF16, region="F")
    sb("sqf", [128, NCC, 256], BF16, region="F")
    sb("sc1f", [128, NCC, 256], BF16, region="F")
    sb("id128f", [128, 128], F32, region="F")
    sb("magt", [128, 1], F32, region="F")
    sb("smg", [1, 64], F32, region="F")
    upk_off = rcur["F"]
    sb("upk", [128, 2, T], BF16, region="F")
    sb("fsf2", [128, max(T, 2048)], F32, region="F")
    end_off = rcur["F"]
    rcur["F"] = upk_off
    sb("dnw", [128, 2, 24, 128], BF16, region="F")
    rcur["F"] = max(rcur["F"], end_off)
    sb("statf", [1, 5, 256], F32, region="F")
    sb("fsf1b", [128, max(T, 2048)], F32, region="F")
    sb("upkb", [128, max(T, 2048)], BF16, region="F")
    assert rcur["F"] <= nc.SBUF_PARTITION_SIZE_BYTES, rcur["F"]
    PS = {i: nc.alloc_psum_tensor(f"pb{i}", [128, 512], F32) for i in range(7)}
    pT = nc.alloc_psum_tensor("pT", [128, 1024], BF16)

    rem = nc.sbuf_bytes_remaining
    print(f"[build] sbuf remaining {rem} bytes/partition")

    Vc = lambda r, cc: A["vecs"][:, cc, r:r + 1]
    bc3 = lambda ap2, n: ap2.rearrange("p (cc u) -> p cc u", u=1).broadcast_to([128, NCC, n])

    for nm in ("rp", "id64d", "id16", "id128", "i64st", "mtril", "mtriu",
               "mtriu0", "ones", "onesr", "cst"):
        b.dma("S", A[nm][:], CT[nm].ap()[:], writes=[A[nm]])
    for cc in range(NCC):
        b.dma("S", A[f"hsum{cc}"][:], CT[f"hsum{cc}"].ap()[:], writes=[A[f"hsum{cc}"]])
        b.dma("S", A[f"hbc{cc}"][:], CT[f"hbc{cc}"].ap()[:], writes=[A[f"hbc{cc}"]])
    b.dma("S", A["vecs"][:], dram["vecs"].rearrange("(cc p) r -> p cc r", p=128),
          writes=[A["vecs"]])
    for nm, row in [("gnw_t", "gn_weight"), ("gnb_t", "gn_bias"),
                    ("gbias_t", "g_lora_bias")]:
        b.dma("G", A[nm][:], dram["vrows"][VR[row]][None, :].broadcast_to([128, C]),
              writes=[A[nm]])
    nc.const_aps.register(F32, 0.0, A["cst"][:, 0:1])
    nc.const_aps.register(F32, 1.0, A["cst"][:, 1:2])
    nc.const_aps.register(F32, EPS, A["cst"][:, 2:3])
    for i in range(7):
        b.memset("V", PS[i][:], 0.0, writes=[PS[i]], tag="pz")
    b.barrier()
    b.tt("V", A["vaux"][:, 0, :], A["vecs"][:, :, VR["k_k"]],
         A["vecs"][:, :, VR["k_k"]], ALU.mult, reads=[A["vecs"]], writes=[A["vaux"]])
    b.ts("V", A["vaux"][:, 1, :], A["vecs"][:, :, VR["k_a"]], -1.0, 1.0,
         ALU.mult, ALU.add, reads=[A["vecs"]], writes=[A["vaux"]])
    for nm, d in [("Wr", "W_r"), ("Wk", "W_k"), ("Wv", "W_v"), ("Wo", "W_o")]:
        b.dma("G", A[nm][:], dram[d].rearrange("(cc p) j -> p cc j", p=128),
              writes=[A[nm]])
    for nm, d in [("wla", "w_lora_a"), ("ala", "a_lora_a"), ("gla", "g_lora_a")]:
        b.dma("G", A[nm][:], dram[d].rearrange("(cc p) r -> p cc r", p=128),
              writes=[A[nm]])
    for nm, d in [("wlb", "w_lora_b"), ("alb", "a_lora_b"), ("glb", "g_lora_b")]:
        b.dma("G", A[nm][:], dram[d][:], writes=[A[nm]])
    yk = [f"Y{h}" for h in range(H)]
    b.memset("V", A["Ybuf"][:], 0.0, writes=[A["Ybuf"]] + yk)

    def dq_rows(view, nb, out_bf):
        n = 32 * nb
        b.red("V", A["dqs"][:, 0:nb], view.rearrange("p (n k) -> p n k", k=32),
              ALU.max, absval=True, reads=[view], writes=[A["dqs"]])
        b.ts("V", A["dqs"][:, 0:nb], A["dqs"][:, 0:nb], 1.0 / 7.0, 1e-12,
             ALU.mult, ALU.max, reads=[A["dqs"]], writes=[A["dqs"]])
        b.recip(A["dqr"][:, 0:nb], A["dqs"][:, 0:nb], reads=[A["dqs"]],
                writes=[A["dqr"]])
        sbc = A["dqs"][:, 0:nb].rearrange("p (n u) -> p n u", u=1).broadcast_to([128, nb, 32])
        rbc = A["dqr"][:, 0:nb].rearrange("p (n u) -> p n u", u=1).broadcast_to([128, nb, 32])
        b.tt("V", A["fsc1"][:, 0:n].rearrange("p (n k) -> p n k", k=32),
             view.rearrange("p (n k) -> p n k", k=32), rbc, ALU.mult,
             reads=[view, A["dqr"]], writes=[A["fsc1"]])
        b.ts("V", A["fsc2"][:, 0:n], A["fsc1"][:, 0:n], MAGIC, MAGIC,
             ALU.add, ALU.subtract, reads=[A["fsc1"]], writes=[A["fsc2"]])
        b.tt("V", out_bf.rearrange("p (n k) -> p n k", k=32),
             A["fsc2"][:, 0:n].rearrange("p (n k) -> p n k", k=32), sbc, ALU.mult,
             reads=[A["fsc2"], A["dqs"]], writes=[out_bf])

    def collective(outs_ap, ins_ap, reads=(), writes=()):
        return b.op("G", lambda e: e.collective_compute(
            "AllGather", ALU.bypass, [list(range(n_cores))], [ins_ap], [outs_ap]),
            reads=reads, writes=writes, tag="AG")

    for src, sh_d in ([("ffn_up_sh", wupT_sh), ("ffn_gate_sh", wgtT_sh)] if ffn_prep else []):
        b.dma("S", A["fsh"][:, 0:NSH, :],
              dram[src].rearrange("(ic p) c -> p ic c", p=128), writes=[A["fsh"]])
        for ic in range(NSH):
            dq_rows(A["fsh"][:, ic, :], 32, A["wqtmp"][:])
            for cc in range(NCC):
                b.trans(pT[:, 0:128], A["wqtmp"][:, 128 * cc:128 * cc + 128],
                        A["id128"][:], reads=[A["wqtmp"], A["id128"]],
                        writes=[pT], tag="dqT")
                b.copy("V", A["fshT"][:, cc, 128 * ic:128 * ic + 128], pT[:, 0:128],
                       reads=[pT], writes=[A["fshT"]])
        b.dma("A", sh_d.rearrange("(cc p) i -> p cc i", p=128), A["fshT"][:],
              reads=[A["fshT"]], writes=[KB.key(sh_d)])
    if ffn_prep:
        collective(wupT_g[:], wupT_sh[:], reads=["wupT_sh"], writes=["wupT_g"])
        collective(wgtT_g[:], wgtT_sh[:], reads=["wgtT_sh"], writes=["wgtT_g"])

        b.dma("S", A["dsh"][:], dram["ffn_down_sh"].rearrange("(u p) i -> p u i", p=HSH)
              [:, 0, :] if HSH == 128 else dram["ffn_down_sh"][:], writes=[A["dsh"]])
        dq_rows(A["dsh"][:, 0:2048], 64, A["dq3"][:, 0:2048])
        dq_rows(A["dsh"][:, 2048:NI], 32, A["dq3"][:, 2048:NI])
        for ic in range(NI // 128):
            srcv = A["dq3"][:, 128 * ic:128 * ic + 128]
            b.trans(pT[:, 0:128], srcv, A["id128"][:],
                    reads=[A["dq3"], A["id128"]], writes=[pT])
            b.copy("V", A["dshT"][:, ic, :], pT[:, 0:HSH], reads=[pT], writes=[A["dshT"]])
        b.dma("A", wdnT_sh.rearrange("(ic p) h -> p ic h", p=128), A["dshT"][:],
              reads=[A["dshT"]], writes=["wdnT_sh"])
        collective(wdnT_g[:], wdnT_sh[:], reads=["wdnT_sh"], writes=["wdnT_g"])
    b.barrier()

    def ln_apply(xin, n_t, out, sq_buf, scr1, scr2, inv_bank, nmi_bank,
                 extra=None, tag="ln", stat_name="stat"):
        st = A[stat_name]
        b.act(sq_buf, xin, AF.Square, reads=[xin], writes=[sq_buf], tag=tag + "sq")
        for cc in range(NCC):
            b.mm(PS[6][0:1, 0:n_t], A["ones"][:], xin[:, cc, :],
                 start=(cc == 0), stop=(cc == NCC - 1),
                 reads=[A["ones"], xin], writes=[PS[6]], tag=tag)
        b.ts("V", st[:, 2, 0:n_t], PS[6][0:1, 0:n_t], 1.0 / C, None, ALU.mult,
             reads=[PS[6]], writes=[st], tag=tag)
        for cc in range(NCC):
            b.mm(PS[6][0:1, 0:n_t], A["ones"][:], sq_buf[:, cc, :],
                 start=(cc == 0), stop=(cc == NCC - 1),
                 reads=[A["ones"], sq_buf], writes=[PS[6]], tag=tag)
        b.ts("V", st[:, 1, 0:n_t], PS[6][0:1, 0:n_t], 1.0 / C, None, ALU.mult,
             reads=[PS[6]], writes=[st], tag=tag)
        b.tt("V", st[:, 0, 0:n_t], st[:, 2, 0:n_t], st[:, 2, 0:n_t], ALU.mult,
             reads=[st], writes=[st], tag=tag)
        b.tt("V", st[:, 1, 0:n_t], st[:, 1, 0:n_t], st[:, 0, 0:n_t], ALU.subtract,
             reads=[st], writes=[st], tag=tag)
        b.act(st[:, 0, 0:n_t], st[:, 1, 0:n_t], AF.Sqrt, bias=EPS,
              reads=[st], writes=[st], tag=tag)
        b.recip(st[:, 3, 0:n_t], st[:, 0, 0:n_t], reads=[st], writes=[st])
        b.stt("V", st[:, 4, 0:n_t], st[:, 2, 0:n_t], -1.0, st[:, 3, 0:n_t],
              ALU.mult, ALU.mult, reads=[st], writes=[st], tag=tag)
        b.mm(inv_bank, A["onesr"][:], st[:, 3, 0:n_t], start=True, stop=True,
             reads=[A["onesr"], st], writes=[inv_bank], tag=tag)
        b.mm(nmi_bank, A["onesr"][:], st[:, 4, 0:n_t], start=True, stop=True,
             reads=[A["onesr"], st], writes=[nmi_bank], tag=tag)
        ib = inv_bank.rearrange("p (u t) -> p u t", u=1).broadcast_to([128, NCC, n_t])
        nb = nmi_bank.rearrange("p (u t) -> p u t", u=1).broadcast_to([128, NCC, n_t])
        b.tt("V", scr1, xin, ib, ALU.mult, reads=[xin, inv_bank], writes=[scr1],
             tag=tag)
        if extra is None:
            b.tt("V", out, scr1, nb, ALU.add, reads=[scr1, nmi_bank], writes=[out],
                 tag=tag)
        else:
            b.tt("V", scr2, scr1, nb, ALU.add, reads=[scr1, nmi_bank], writes=[scr2],
                 tag=tag)
            b.tt("V", out, scr2, extra, ALU.add, reads=[scr2, extra], writes=[out],
                 tag=tag)

    def fm_proj(out_buf, w_buf, rhs_buf, evac, tag, nb=3):
        for jj in range(NCC):
            pb = PS[jj % nb]
            for cc in range(NCC):
                b.mm(pb[:, 0:W], w_buf[:, cc, 128 * jj:128 * jj + 128],
                     rhs_buf[:, cc, 0:W], start=(cc == 0), stop=(cc == NCC - 1),
                     reads=[w_buf, rhs_buf], writes=[pb], tag=tag)
            evac(jj, pb)

    def emit_loads(w):
        t0 = w * W
        wk = f"w{w}"
        xbb = A[f"xb{w % 2}"]
        if w == 0:
            b.dma("G", xbb[:, :, 1:W1],
                  xT_d.rearrange("(cc p) t -> p cc t", p=128)[:, :, 0:W],
                  writes=[xbb], tag=wk)
            b.memset("V", xbb[:, :, 0:1], 0.0, writes=[xbb], tag=wk)
        else:
            b.dma("G", xbb[:],
                  xT_d.rearrange("(cc p) t -> p cc t", p=128)[:, :, t0 - 1:t0 + W],
                  writes=[xbb], tag=wk)
        b.dma("S", A[f"xf{w % 2}"][:],
              xT_d.rearrange("(cc p) t -> p cc t", p=128)[:, :, t0:t0 + W],
              writes=[A[f"xf{w % 2}"]], tag=wk)

    def emit_prep(w):
        t0 = w * W
        wk = f"w{w}"
        xbb = A[f"xb{w % 2}"]
        ln_apply(xbb[:], W1, A["hw"][:], A["sq"][:], A["lnsA"][:], None,
                 PS[5][:, 0:W1], PS[4][:, 0:W1], tag=wk + "ln1")
        if w == 0:
            b.memset("V", A["hw"][:, :, 0:1], 0.0, writes=[A["hw"]], tag=wk)
        b.tt("V", A["dl"][:], A["hw"][:, :, 0:W], A["hw"][:, :, 1:W1],
             ALU.subtract, reads=[A["hw"]], writes=[A["dl"]], tag=wk + "dl")

        def mix_into(buf, vrow, eng):
            vb = A["vecs"][:, :, vrow].rearrange("p (cc u) -> p cc u", u=1)\
                .broadcast_to([128, NCC, W])
            scr = A["f32a"][:] if eng == "V" else A["f32b"][:]
            b.tt(eng, scr, A["dl"][:], vb, ALU.mult,
                 reads=[A["dl"], A["vecs"]], writes=[scr], tag=wk + "mx")
            b.tt(eng, buf, scr, A["hw"][:, :, 1:W1], ALU.add,
                 reads=[scr, A["hw"]], writes=[buf], tag=wk + "mx")

        def evac_copy(buf):
            def f(jj, pb):
                b.ts("V", buf[:, jj, 0:W], pb[:, 0:W], 0.0, None, ALU.add,
                     reads=[pb], writes=[buf], tag=wk + "pe")
            return f

        mix_into(A["mixA"][:], VR["x_r"], "V")
        fm_proj(A["rw"], A["Wr"], A["mixA"], evac_copy(A["rw"]), wk + "pr")
        mix_into(A["mixB"][:], VR["x_k"], "G")
        fm_proj(A["kw"], A["Wk"], A["mixB"], evac_copy(A["kw"]), wk + "pk")
        mix_into(A["mixA"][:], VR["x_v"], "V")
        fm_proj(A["vw"], A["Wv"], A["mixA"], evac_copy(A["vw"]), wk + "pv")

        mix_into(A["mixB"][:], VR["x_w"], "G")
        for cc in range(NCC):
            b.mm(PS[2][0:64, 0:W], A["wla"][:, cc, :], A["mixB"][:, cc, :],
                 start=(cc == 0), stop=(cc == NCC - 1),
                 reads=[A["wla"], A["mixB"]], writes=[PS[2]], tag=wk + "wl1")
        b.act(A["th64"][:, 0:W], PS[2][0:64, 0:W], AF.Tanh, reads=[PS[2]],
              writes=[A["th64"]], tag=wk + "wl1e")
        for jj in range(NCC):
            pb = PS[jj % 2]
            b.mm(pb[:, 0:W], A["wlb"][0:64, 128 * jj:128 * jj + 128],
                 A["th64"][0:64, 0:W], start=True, stop=True,
                 reads=[A["wlb"], A["th64"]], writes=[pb], tag=wk + "wl2")
            b.act(A["f32a"][:, jj, :], pb[:, 0:W], AF.Identity,
                  bias=Vc(VR["w_lora_bias"], jj), reads=[pb, A["vecs"]],
                  writes=[A["f32a"]], tag=wk + "wl2e")
        b.act(A["f32b"][:], A["f32a"][:], AF.Exp, scale=-1.0, reads=[A["f32a"]],
              writes=[A["f32b"]], tag=wk + "sp")
        b.act(A["f32a"][:], A["f32b"][:], AF.Ln, bias=1.0, reads=[A["f32b"]],
              writes=[A["f32a"]], tag=wk + "sp")
        b.ts("V", A["logw"][:], A["f32a"][:], -1.0, -0.5, ALU.mult, ALU.add,
             reads=[A["f32a"]], writes=[A["logw"]], tag=wk + "sp")
        mix_into(A["mixB"][:], VR["x_a"], "G")
        for cc in range(NCC):
            b.mm(PS[2][0:64, 0:W], A["ala"][:, cc, :], A["mixB"][:, cc, :],
                 start=(cc == 0), stop=(cc == NCC - 1),
                 reads=[A["ala"], A["mixB"]], writes=[PS[2]], tag=wk + "al1")
        b.copy("A", A["th64"][:, 0:W], PS[2][0:64, 0:W], reads=[PS[2]],
               writes=[A["th64"]], tag=wk + "al1e")
        for jj in range(NCC):
            pb = PS[jj % 2]
            b.mm(pb[:, 0:W], A["alb"][0:64, 128 * jj:128 * jj + 128],
                 A["th64"][0:64, 0:W], start=True, stop=True,
                 reads=[A["alb"], A["th64"]], writes=[pb], tag=wk + "al2")
            b.act(A["aw"][:, jj, :], pb[:, 0:W], AF.Sigmoid,
                  bias=Vc(VR["a_lora_bias"], jj), reads=[pb, A["vecs"]],
                  writes=[A["aw"]], tag=wk + "al2e")
        for cc in range(NCC):
            b.scan_op("V", A["gw"][:, cc, :], A["rp"][:], A["logw"][:, cc, :], 0.0,
                      ALU.mult, ALU.add, reads=[A["rp"], A["logw"]],
                      writes=[A["gw"]], tag=wk + "gs")
        b.tt("G", A["f32a"][:], A["gw"][:], A["logw"][:], ALU.subtract,
             reads=[A["gw"], A["logw"]], writes=[A["f32a"]], tag=wk + "gp")
        b.act(A["eg"][:], A["gw"][:], AF.Exp, reads=[A["gw"]], writes=[A["eg"]],
              tag=wk + "eg")
        b.act(A["emg"][:], A["gw"][:], AF.Exp, scale=-1.0, reads=[A["gw"]],
              writes=[A["emg"]], tag=wk + "emg")
        b.act(A["egp"][:], A["f32a"][:], AF.Exp, reads=[A["f32a"]],
              writes=[A["egp"]], tag=wk + "egp")
        b.act(A[f"lam{w % 2}"][:],
              A["gw"][:].rearrange("p cc (ch l) -> p cc ch l", l=L)[:, :, :, L - 1],
              AF.Exp, reads=[A["gw"]], writes=[A[f"lam{w % 2}"]], tag=wk + "lam")
        b.copy("V", A[f"lamp{w % 2}"][:, :, 1:NCH], A[f"lam{w % 2}"][:, :, 0:NCH - 1],
               reads=[A[f"lam{w % 2}"]], writes=[A[f"lamp{w % 2}"]], tag=wk + "lp")
        b.memset("V", A[f"lamp{w % 2}"][:, :, 0:1], 1.0, writes=[A[f"lamp{w % 2}"]], tag=wk + "lp")
        kab = bc3(A["vecs"][:, :, VR["k_a"]], W)
        okab = bc3(A["vaux"][:, 1, :], W)
        b.tt("G", A["f32b"][:], A["aw"][:], kab, ALU.mult,
             reads=[A["aw"], A["vecs"]], writes=[A["f32b"]], tag=wk + "m1")
        b.tt("G", A["m1"][:], A["f32b"][:], okab, ALU.add,
             reads=[A["f32b"], A["vaux"]], writes=[A["m1"]], tag=wk + "m1")
        b.tt("V", A["k2"][:], A["kw"][:], A["m1"][:], ALU.mult,
             reads=[A["kw"], A["m1"]], writes=[A["k2"]], tag=wk + "k2")
        b.act(A["sq"][:, :, 0:W], A["kw"][:], AF.Square, reads=[A["kw"]],
              writes=[A["sq"]], tag=wk + "ks")
        b.tt("V", A["mixB"][:], A["sq"][:, :, 0:W], bc3(A["vaux"][:, 0, :], W),
             ALU.mult, reads=[A["sq"], A["vaux"]], writes=[A["mixB"]], tag=wk + "ks")
        for cc in range(NCC):
            b.mm(PS[2][0:16, 0:W], A[f"hsum{cc}"][:], A["mixB"][:, cc, :],
                 start=(cc == 0), stop=(cc == NCC - 1),
                 reads=[A[f"hsum{cc}"], A["mixB"]], writes=[PS[2]], tag=wk + "kn")
        b.act(A["kn16"][:, 0:W], PS[2][0:16, 0:W], AF.Sqrt, reads=[PS[2]],
              writes=[A["kn16"]], tag=wk + "kn")
        b.ts("V", A["kn16b"][:, 0:W], A["kn16"][:, 0:W], 1e-12, None, ALU.max,
             reads=[A["kn16"]], writes=[A["kn16b"]], tag=wk + "kn")
        b.recip(A["kn16"][:, 0:W], A["kn16b"][:, 0:W], reads=[A["kn16b"]],
                writes=[A["kn16"]])
        b.copy("V", A["kinvb"][:, 0:W], A["kn16"][:, 0:W], reads=[A["kn16"]],
               writes=[A["kinvb"]], tag=wk + "kn")
        for cc in range(NCC):
            b.mm(PS[6][:, 0:W], A[f"hbc{cc}"][:], A["kinvb"][0:16, 0:W],
                 start=True, stop=True, reads=[A[f"hbc{cc}"], A["kinvb"]],
                 writes=[PS[6]], tag=wk + "kb")
            b.stt("V", A["kkw"][:, cc, :], A["kw"][:, cc, :], Vc(VR["k_k"], cc),
                  PS[6][:, 0:W], ALU.mult, ALU.mult,
                  reads=[A["kw"], A["vecs"], PS[6]], writes=[A["kkw"]], tag=wk + "kk")
        b.tt("V", A["braw"][:], A["kkw"][:], A["aw"][:], ALU.mult,
             reads=[A["kkw"], A["aw"]], writes=[A["braw"]], tag=wk + "br")
        for cc in range(NCC):
            b.stt("V", A["rkp"][:, cc, :], A["rw"][:, cc, :], Vc(VR["r_k"], cc),
                  A["k2"][:, cc, :], ALU.mult, ALU.mult,
                  reads=[A["rw"], A["vecs"], A["k2"]], writes=[A["rkp"]],
                  tag=wk + "rk")
        for cc in range(NCC):
            b.mm(PS[2][0:16, 0:W], A[f"hsum{cc}"][:], A["rkp"][:, cc, :],
                 start=(cc == 0), stop=(cc == NCC - 1),
                 reads=[A[f"hsum{cc}"], A["rkp"]], writes=[PS[2]], tag=wk + "bt")
        b.copy("V", A["beta"][:, 0:W], PS[2][0:16, 0:W], reads=[PS[2]],
               writes=[A["beta"]], tag=wk + "bt")
        for tt in range(NT):
            b.trans(pT[:, 512:528], A["beta"][0:16, 128 * tt:128 * tt + 128],
                    A["id16"][:], reads=[A["beta"], A["id16"]], writes=[pT],
                    tag=wk + "btT")
            b.copy("V", A[f"betat{w % 2}"][:, tt, :], pT[:, 512:528], reads=[pT],
                   writes=[A[f"betat{w % 2}"]], tag=wk + "btT")
        for cc in range(NCC):
            b.mm(PS[4][:, 0:W], A["gla"][:, cc, :], A["rw"][:, cc, :],
                 start=(cc == 0), stop=(cc == NCC - 1),
                 reads=[A["gla"], A["rw"]], writes=[PS[4]], tag=wk + "g1")
        b.act(A["sg"][:, 0:W], PS[4][:, 0:W], AF.Sigmoid, reads=[PS[4]],
              writes=[A["sg"]], tag=wk + "g1e")
        for tt in range(NT):
            for hf in range(2):
                pb = PS[hf]
                b.mm(pb[:, 0:512], A["sg"][:, 128 * tt:128 * tt + 128],
                     A["glb"][:, 512 * hf:512 * hf + 512], start=True, stop=True,
                     reads=[A["sg"], A["glb"]], writes=[pb], tag=wk + "g2")
                b.tt("V", A[f"ggate{w % 2}"][:, tt, 512 * hf:512 * hf + 512], pb[:, 0:512],
                     A["gbias_t"][:, 512 * hf:512 * hf + 512], ALU.add,
                     reads=[pb, A["gbias_t"]], writes=[A[f"ggate{w % 2}"]], tag=wk + "g2e")
        scv = lambda t: A["SC"][:, :, t, :, :]
        shp = lambda x: x.rearrange("p cc (ch l) -> p cc ch l", l=L)
        b.tt("V", scv(0), shp(A["braw"][:]), shp(A["emg"][:]), ALU.mult,
             reads=[A["braw"], A["emg"]], writes=[A["SC"]], tag=wk + "sc")
        b.tt("V", scv(1), shp(A["k2"][:]), shp(A["emg"][:]), ALU.mult,
             reads=[A["k2"], A["emg"]], writes=[A["SC"]], tag=wk + "sc")
        b.tt("G", scv(2), shp(A["rw"][:]), shp(A["eg"][:]), ALU.mult,
             reads=[A["rw"], A["eg"]], writes=[A["SC"]], tag=wk + "sc")
        b.stt("V", scv(3), shp(A["kkw"][:]), -1.0, shp(A["egp"][:]), ALU.mult,
              ALU.mult, reads=[A["kkw"], A["egp"]], writes=[A["SC"]], tag=wk + "sc")

    def emit_scan(w):
        wk = f"w{w}"
        if only_head is not None:
            okeys0 = [f"Otm_{ch}_{hh}" for ch in range(NCH) for hh in range(H)]
            vkeys0 = [f"Vtm_{cg}_{hh}" for cg in range(NCG) for hh in range(H)]
            b.memset("V", A["Otm"][:], 0.0, writes=[A["Otm"]] + okeys0, tag=wk)
            b.memset("V", A["Vtm"][:], 0.0, writes=[A["Vtm"]] + vkeys0, tag=wk)
            for cg in range(NCG):
                emit_quad(b, A, PS, pT, cg, only_head, wk, w % 2)
                for ch in range(4 * cg, 4 * cg + 4):
                    emit_seq(b, A, PS, ch, only_head, NCH, wk, w % 2)
            return
        import os as _os2
        MODE = _os2.environ.get("SCANMODE", "SER")
        if MODE.startswith("SER") and MODE != "SER":
            G = int(MODE[3:])
            hg = 0
            while hg < H:
                hs_grp = list(range(hg, min(hg + G, H)))
                for cg in range(NCG):
                    for hh in hs_grp:
                        emit_quad(b, A, PS, pT, cg, hh, wk, w % 2)
                    for ch in range(4 * cg, 4 * cg + 4):
                        for hh in hs_grp:
                            emit_seq(b, A, PS, ch, hh, NCH, wk, w % 2)
                hg += G
            return
        if MODE == "SER":
            for h0 in range(H):
                for cg in range(NCG):
                    emit_quad(b, A, PS, pT, cg, h0, wk, w % 2)
                    for ch in range(4 * cg, 4 * cg + 4):
                        emit_seq(b, A, PS, ch, h0, NCH, wk, w % 2)
        elif MODE == "QP":
            for h0 in range(0, H, 2):
                for cg in range(NCG):
                    emit_quad(b, A, PS, pT, cg, h0, wk, w % 2)
                    emit_quad(b, A, PS, pT, cg, h0 + 1, wk, w % 2)
                    for ch in range(4 * cg, 4 * cg + 4):
                        emit_seq(b, A, PS, ch, h0, NCH, wk, w % 2)
                    for ch in range(4 * cg, 4 * cg + 4):
                        emit_seq(b, A, PS, ch, h0 + 1, NCH, wk, w % 2)
        elif MODE.startswith("QPX"):
            skip = int(MODE[3:])
            for h0 in range(0, H, 2):
                for cg in range(NCG):
                    if h0 // 2 != skip:
                        emit_quad(b, A, PS, pT, cg, h0, wk, w % 2)
                        emit_quad(b, A, PS, pT, cg, h0 + 1, wk, w % 2)
                        for ch in range(4):
                            emit_seq(b, A, PS, ch, h0, NCH, wk, w % 2)
                        for ch in range(4):
                            emit_seq(b, A, PS, ch, h0 + 1, NCH, wk, w % 2)
                    else:
                        for hh in (h0, h0 + 1):
                            emit_quad(b, A, PS, pT, cg, hh, wk, w % 2)
                            for ch in range(4 * cg, 4 * cg + 4):
                                emit_seq(b, A, PS, ch, hh, NCH, wk, w % 2)
        elif MODE.startswith("QPO"):
            which = int(MODE[3:])
            for h0 in range(0, H, 2):
                for cg in range(NCG):
                    if h0 // 2 == which:
                        emit_quad(b, A, PS, pT, cg, h0, wk, w % 2)
                        emit_quad(b, A, PS, pT, cg, h0 + 1, wk, w % 2)
                        for ch in range(4):
                            emit_seq(b, A, PS, ch, h0, NCH, wk, w % 2)
                        for ch in range(4):
                            emit_seq(b, A, PS, ch, h0 + 1, NCH, wk, w % 2)
                    else:
                        for hh in (h0, h0 + 1):
                            emit_quad(b, A, PS, pT, cg, hh, wk, w % 2)
                            for ch in range(4 * cg, 4 * cg + 4):
                                emit_seq(b, A, PS, ch, hh, NCH, wk, w % 2)
        elif MODE.startswith("QPN"):
            npair = int(MODE[3:])
            for h0 in range(0, H, 2):
                for cg in range(NCG):
                    if h0 // 2 < npair:
                        emit_quad(b, A, PS, pT, cg, h0, wk, w % 2)
                        emit_quad(b, A, PS, pT, cg, h0 + 1, wk, w % 2)
                        for ch in range(4):
                            emit_seq(b, A, PS, ch, h0, NCH, wk, w % 2)
                        for ch in range(4):
                            emit_seq(b, A, PS, ch, h0 + 1, NCH, wk, w % 2)
                    else:
                        for hh in (h0, h0 + 1):
                            emit_quad(b, A, PS, pT, cg, hh, wk, w % 2)
                            for ch in range(4 * cg, 4 * cg + 4):
                                emit_seq(b, A, PS, ch, hh, NCH, wk, w % 2)
        else:
            for h0 in range(0, H, 2):
                for cg in range(NCG):
                    emit_quad(b, A, PS, pT, cg, h0, wk, w % 2)
                    emit_quad(b, A, PS, pT, cg, h0 + 1, wk, w % 2)
                    for ch in range(4 * cg, 4 * cg + 4):
                        emit_seq(b, A, PS, ch, h0, NCH, wk, w % 2)
                        emit_seq(b, A, PS, ch, h0 + 1, NCH, wk, w % 2)
    def emit_post(w):
        t0 = w * W
        wk = f"w{w}"
        ov = A["Otm"][:].rearrange("p tt (hh d) -> p tt hh d", d=64)
        okeys = [f"Otm_{ch}_{h}" for ch in range(NCH) for h in range(H)]
        vkeys = [f"Vtm_{cg}_{h}" for cg in range(NCG) for h in range(H)]
        b.red("V", A["gn16"][:], ov, ALU.add, reads=[A["Otm"]] + okeys,
              writes=[A["gn16"]], tag=wk + "gn")
        b.act(A["goTM"][:], A["Otm"][:], AF.Square, reads=[A["Otm"]] + okeys,
              writes=[A["goTM"]], tag=wk + "gn")
        b.red("V", A["gn16b"][:],
              A["goTM"][:].rearrange("p tt (hh d) -> p tt hh d", d=64), ALU.add,
              reads=[A["goTM"]], writes=[A["gn16b"]], tag=wk + "gn")
        b.ts("V", A["gn16"][:], A["gn16"][:], 1.0 / 64, None, ALU.mult,
             reads=[A["gn16"]], writes=[A["gn16"]], tag=wk + "gn")
        b.ts("V", A["gn16b"][:], A["gn16b"][:], 1.0 / 64, None, ALU.mult,
             reads=[A["gn16b"]], writes=[A["gn16b"]], tag=wk + "gn")
        b.tt("V", A["gnA"][:], A["gn16"][:], A["gn16"][:], ALU.mult,
             reads=[A["gn16"]], writes=[A["gnA"]], tag=wk + "gn")
        b.tt("V", A["gnB"][:], A["gn16b"][:], A["gnA"][:], ALU.subtract,
             reads=[A["gn16b"], A["gnA"]], writes=[A["gnB"]], tag=wk + "gn")
        b.act(A["gnB"][:], A["gnB"][:], AF.Sqrt, bias=EPS, reads=[A["gnB"]],
              writes=["gnBs"], tag=wk + "gn")
        b.recip(A["gnA"][:], A["gnB"][:], reads=["gnBs"], writes=[A["gnA"]])
        b.stt("V", A["gnB"][:], A["gn16"][:], -1.0, A["gnA"][:], ALU.mult, ALU.mult,
              reads=[A["gn16"], A["gnA"]], writes=[A["gnB"]], tag=wk + "gn")
        gA = A["gnA"][:].rearrange("p tt (hh u) -> p tt hh u", u=1)\
            .broadcast_to([128, NT, 16, 64])
        gB = A["gnB"][:].rearrange("p tt (hh u) -> p tt hh u", u=1)\
            .broadcast_to([128, NT, 16, 64])
        gv = A["goTM"][:].rearrange("p tt (hh d) -> p tt hh d", d=64)
        b.tt("V", gv, ov, gA, ALU.mult, reads=[A["Otm"], A["gnA"]],
             writes=[A["goTM"]], tag=wk + "gna")
        b.tt("V", A["mixA"][:].rearrange("p cc w -> p cc w") if False else
             A["goTM"][:].rearrange("p tt (hh d) -> p tt hh d", d=64),
             gv, gB, ALU.add, reads=[A["goTM"], A["gnB"]], writes=[A["goTM"]],
             tag=wk + "gna2")
        gnwb = A["gnw_t"][:].rearrange("p (u c) -> p u c", u=1).broadcast_to([128, NT, C])
        gnbb = A["gnb_t"][:].rearrange("p (u c) -> p u c", u=1).broadcast_to([128, NT, C])
        b.tt("G", A["goTM"][:], A["goTM"][:], gnwb, ALU.mult,
             reads=[A["goTM"], A["gnw_t"]], writes=[A["goTM"]], tag=wk + "gnw")
        b.tt("G", A["goTM"][:], A["goTM"][:], gnbb, ALU.add,
             reads=[A["goTM"], A["gnb_t"]], writes=[A["goTM"]], tag=wk + "gnb")
        bb = A[f"betat{w % 2}"][:].rearrange("p tt (hh u) -> p tt hh u", u=1)\
            .broadcast_to([128, NT, 16, 64])
        bsc = A["mixB"][:].rearrange("p cc w -> p (cc w)")\
            .rearrange("p (tt hh d) -> p tt hh d", tt=NT, d=64)
        b.tt("V", bsc, A["Vtm"][:].rearrange("p tt (hh d) -> p tt hh d", d=64),
             bb, ALU.mult,
             reads=[A["Vtm"], A[f"betat{w % 2}"]] + vkeys, writes=[A["mixB"]],
             tag=wk + "bo")
        b.tt("V", A["goTM"][:].rearrange("p tt (hh d) -> p tt hh d", d=64),
             A["goTM"][:].rearrange("p tt (hh d) -> p tt hh d", d=64), bsc, ALU.add,
             reads=[A["goTM"], A["mixB"]], writes=[A["goTM"]], tag=wk + "bo")
        b.tt("V", A["goTM"][:], A["goTM"][:], A[f"ggate{w % 2}"][:], ALU.mult,
             reads=[A["goTM"], A[f"ggate{w % 2}"]], writes=[A["goTM"]], tag=wk + "ga")
        for tt in range(NT):
            for cc in range(NCC):
                b.trans(pT[:, 640:768], A["goTM"][:, tt, 128 * cc:128 * cc + 128],
                        A["id128"][:], reads=[A["goTM"], A["id128"]], writes=[pT],
                        tag=wk + "gT")
                b.copy("A", A["goFM"][:, cc, 128 * tt:128 * tt + 128], pT[:, 640:768],
                       reads=[pT], writes=[A["goFM"]], tag=wk + "gT")
        def evac_att(jj, pb):
            b.copy("V" if jj % 2 == 0 else "A", A["attw"][:, jj, 0:W], pb[:, 0:W],
                   reads=[pb], writes=[A["attw"]], tag=wk + "woe")
        fm_proj(A["attw"], A["Wo"], A["goFM"], evac_att, wk + "wo")
        ln_apply(A["attw"][:], W, A["xrnew"][:], A["sq"][:, :, 0:W],
                 A["lnsA"][:, :, 0:W], A["mixA"][:], PS[5][:, 0:W], PS[4][:, 0:W],
                 extra=A[f"xf{w % 2}"][:], tag=wk + "al")
        b.dma("S", xres_d.rearrange("(cc p) t -> p cc t", p=128)[:, :, t0:t0 + W],
              A["xrnew"][:], reads=[A["xrnew"]], writes=["xres_dram"], tag=wk + "xr")

    if False:
        for w in range(NW):
            emit_prep(w)
            emit_scan(w)
            emit_post(w)
    else:
        emit_loads(0)
        emit_prep(0)
        for w in range(NW):
            if w + 1 < NW:
                emit_loads(w + 1)
            emit_scan(w)
            if w + 1 < NW:
                emit_prep(w + 1)
            emit_post(w)

    b.barrier()
    if upto == "attn":
        for blk in range(max(T // 256, 1)):
            BS = min(256, T)
            c0 = blk * BS
            b.dma("G", A["fres"][:, :, 0:BS],
                  xres_d.rearrange("(cc p) t -> p cc t", p=128)[:, :, c0:c0 + BS],
                  reads=["xres_dram"], writes=[A["fres"]], tag="out")
            b.dma("S", y_d.rearrange("(cc p) t -> p cc t", p=128)[:, :, c0:c0 + BS],
                  A["fres"][:, :, 0:BS], reads=[A["fres"]], writes=["y_out"], tag="out")
    else:
        ffn_phase(b, A, PS, pT, T, n_cores, NSH, HSH, dram, CT, xres_d, up_sp_d,
                  wupT_g, wgtT_g, wdnT_g, mx_in, mx_out, scal_d, y_d, collective,
                  ln_apply)
    for tname in tap_d:
        src_ap, rd = TAPS[tname](A)
        b.dma("S", tap_d[tname], src_ap, reads=rd, writes=[f"tap_{tname}"])
    b.barrier()
    with nc.Block() as block:
        b.realize(block)
    return nc, A, dram, tap_d


def emit_quad(b, A, PS, pT, cg, h, wk, wpar=0):
    cc, half = h // 2, 64 * (h % 2)
    par = (cg * 16 + h) % 2
    par4 = h % 4
    eqp = (cg * 16 + h) % 4
    p2 = h % 2
    SCORE, AUX, PW = PS[4 * p2], PS[1 + 4 * p2], PS[2 + 4 * p2]
    tg = f"{wk}q{cg}h{h}"
    SCs = lambda ty, ch: A["SC"][half:half + 64, cc, ty, ch, :]
    SC4 = lambda ty: A["SC"][half:half + 64, cc, ty, 4 * cg:4 * cg + 4, :]\
        .rearrange("p a b -> p (a b)")
    b.mm(SCORE[:, 0:128], SC4(3), SC4(0), start=True, stop=True, tp=(half, 0),
         reads=[A["SC"]], writes=[SCORE], tag=tg + "T")
    b.mm(SCORE[:, 128:256], SC4(0), SC4(3), start=True, stop=True, tp=(half, 0),
         reads=[A["SC"]], writes=[SCORE], tag=tg + "A")
    b.mm(SCORE[:, 256:384], SC4(1), SC4(3), start=True, stop=True, tp=(half, 0),
         reads=[A["SC"]], writes=[SCORE], tag=tg + "M")
    b.mm(SCORE[:, 384:512], SC4(1), SC4(2), start=True, stop=True, tp=(half, 0),
         reads=[A["SC"]], writes=[SCORE], tag=tg + "K")
    b.mm(AUX[:, 0:128], SC4(0), SC4(2), start=True, stop=True, tp=(half, 0),
         reads=[A["SC"]], writes=[AUX], tag=tg + "B")
    b.tt("V", A[f"Tq{par}"][:], SCORE[:, 0:128], A["mtril"][:], ALU.mult,
         reads=[SCORE, A["mtril"]], writes=[A[f"Tq{par}"]], tag=tg + "e")
    b.tt("V", A[f"Aq{par}"][:], SCORE[:, 128:256], A["mtriu"][:], ALU.mult,
         reads=[SCORE, A["mtriu"]], writes=[A[f"Aq{par}"]], tag=tg + "e")
    b.tt("V", A[f"Mq{par}"][:], SCORE[:, 256:384], A["mtriu"][:], ALU.mult,
         reads=[SCORE, A["mtriu"]], writes=[A[f"Mq{par}"]], tag=tg + "e")
    b.tt("V", A[f"Kq{par}"][:], SCORE[:, 384:512], A["mtriu0"][:], ALU.mult,
         reads=[SCORE, A["mtriu0"]], writes=[A[f"Kq{par}"]], tag=tg + "e")
    b.tt("V", A[f"Bq{par}"][:], AUX[:, 0:128], A["mtriu0"][:], ALU.mult,
         reads=[AUX, A["mtriu0"]], writes=[A[f"Bq{par}"]], tag=tg + "e")
    pb0 = 256 * p2
    ptk = f"pT{p2}"
    idd = A["id64d"][half:half + 64, :]
    b.trans(pT[:, pb0 + 0:pb0 + 64], SC4(1), idd, tp=(half, 0),
            reads=[A["SC"], A["id64d"]], writes=[ptk], tag=tg + "t")
    b.trans(pT[:, pb0 + 64:pb0 + 128], SC4(0), idd, tp=(half, 0),
            reads=[A["SC"], A["id64d"]], writes=[ptk], tag=tg + "t")
    b.trans(pT[:, pb0 + 128:pb0 + 192], A["vw"][half:half + 64, cc, 128 * cg:128 * cg + 128],
            idd, tp=(half, 0), reads=[A["vw"], A["id64d"]], writes=[ptk], tag=tg + "t")
    b.trans(pT[:, pb0 + 192:pb0 + 256], SC4(3), idd, tp=(half, 0),
            reads=[A["SC"], A["id64d"]], writes=[ptk], tag=tg + "t")
    b.copy("A", A[f"kbv{par4}"][:], pT[:, pb0 + 0:pb0 + 192], reads=[ptk],
           writes=[A[f"kbv{par4}"]], tag=tg + "kb")
    b.copy("A", A["Vtm"][:, cg, 64 * h:64 * h + 64], pT[:, pb0 + 128:pb0 + 192],
           reads=[ptk], writes=[f"Vtm_{cg}_{h}"], tag=tg + "vt")
    b.copy("A", A[f"rhs{par}"][:, 0:64], pT[:, pb0 + 192:pb0 + 256], reads=[ptk],
           writes=[A[f"rhs{par}"]], tag=tg + "rh")
    b.mm(AUX[:, 128:192], A[f"Mq{par}"][:], A[f"kbv{par4}"][:, 128:192],
         start=True, stop=True, reads=[A[f"Mq{par}"], A[f"kbv{par4}"]],
         writes=[AUX], tag=tg + "MV")
    b.copy("V", A[f"rhs{par}"][:, 64:128], AUX[:, 128:192], reads=[AUX],
           writes=[A[f"rhs{par}"]], tag=tg + "rh2")
    zs = [A[f"rhs{par}"], A[f"Za{par4}"], A[f"Zb{par}"], A[f"Zc{par}"],
          A[f"rhs{par}"], A[f"Za{par4}"]]
    def apply_i(i, pow_ap, pow_reads):
        b.mm(SCORE[:, 0:128], pow_ap, zs[i][:], start=True, stop=True,
             reads=pow_reads + [zs[i]], writes=[SCORE], tag=tg + f"a{i}")
        b.tt("V", zs[i + 1][:], SCORE[:, 0:128], zs[i][:], ALU.add,
             reads=[SCORE, zs[i]], writes=[zs[i + 1]], tag=tg + f"a{i}e")
    NOINTER = True
    pend = []
    def apply_d(i, pow_ap, pow_reads):
        if NOINTER:
            pend.append((i, pow_ap, pow_reads))
        else:
            apply_i(i, pow_ap, pow_reads)
    apply_d(0, A[f"Aq{par}"][:], [A[f"Aq{par}"]])
    b.mm(PW[:, 0:128], A[f"Aq{par}"][:], A[f"Tq{par}"][:], start=True, stop=True,
         reads=[A[f"Aq{par}"], A[f"Tq{par}"]], writes=[PW], tag=tg + "p2")
    b.mm(PW[:, 128:256], A[f"Tq{par}"][:], A[f"Aq{par}"][:], start=True, stop=True,
         reads=[A[f"Aq{par}"], A[f"Tq{par}"]], writes=[PW], tag=tg + "p2")
    b.ts("V", A[f"P2{par}"][:], PW[:, 0:256], 0.0, None, ALU.add,
         reads=[PW], writes=[A[f"P2{par}"]], tag=tg + "p2e")
    apply_d(1, A[f"P2{par}"][:, 128:256], [A[f"P2{par}"]])
    b.mm(PW[:, 256:384], A[f"P2{par}"][:, 128:256], A[f"P2{par}"][:, 0:128],
         start=True, stop=True, reads=[A[f"P2{par}"]], writes=[PW], tag=tg + "p4")
    b.mm(PW[:, 384:512], A[f"P2{par}"][:, 0:128], A[f"P2{par}"][:, 128:256],
         start=True, stop=True, reads=[A[f"P2{par}"]], writes=[PW], tag=tg + "p4")
    b.ts("V", A[f"P4{par}"][:], PW[:, 256:512], 0.0, None, ALU.add,
         reads=[PW], writes=[A[f"P4{par}"]], tag=tg + "p4e")
    apply_d(2, A[f"P4{par}"][:, 128:256], [A[f"P4{par}"]])
    b.mm(PW[:, 0:128], A[f"P4{par}"][:, 128:256], A[f"P4{par}"][:, 0:128],
         start=True, stop=True, reads=[A[f"P4{par}"]], writes=[PW], tag=tg + "p8")
    b.mm(PW[:, 128:256], A[f"P4{par}"][:, 0:128], A[f"P4{par}"][:, 128:256],
         start=True, stop=True, reads=[A[f"P4{par}"]], writes=[PW], tag=tg + "p8")
    b.ts("V", A[f"P8{par}"][:], PW[:, 0:256], 0.0, None, ALU.add,
         reads=[PW], writes=[A[f"P8{par}"]], tag=tg + "p8e")
    apply_d(3, A[f"P8{par}"][:, 128:256], [A[f"P8{par}"]])
    b.mm(PW[:, 256:384], A[f"P8{par}"][:, 0:128], A[f"P8{par}"][:, 128:256],
         start=True, stop=True, reads=[A[f"P8{par}"]], writes=[PW], tag=tg + "p16")
    b.ts("V", A[f"P16{par}"][:], PW[:, 256:384], 0.0, None, ALU.add,
         reads=[PW], writes=[A[f"P16{par}"]], tag=tg + "p16e")
    apply_d(4, A[f"P16{par}"][:], [A[f"P16{par}"]])
    for args in pend:
        apply_i(*args)
    Z5 = A[f"Za{par4}"]
    for r in range(4):
        row = slice(32 * r, 32 * r + 32)
        b.mm(AUX[row, 320:384], A[f"Kq{par}"][row, 32 * r:32 * r + 32],
             A["Vtm"][row, cg, 64 * h:64 * h + 64], start=True, stop=False,
             tp=(32 * r, 32 * r), reads=[A[f"Kq{par}"], f"Vtm_{cg}_{h}"],
             writes=[AUX], tag=tg + "oi")
        b.mm(AUX[row, 320:384], A[f"Bq{par}"][row, 32 * r:32 * r + 32],
             Z5[row, 64:128], start=False, stop=True, tp=(32 * r, 32 * r),
             reads=[A[f"Bq{par}"], Z5], writes=[AUX], tag=tg + "oi")
    b.copy("V", A[f"OI{par4}"][:], AUX[:, 320:384], reads=[AUX],
           writes=[A[f"OI{par4}"]], tag=tg + "oie")
    for r in range(4):
        row = slice(32 * r, 32 * r + 32)
        b.mm(PW[half:half + 64, 96 * r:96 * r + 64], Z5[row, 0:64],
             A[f"kbv{par4}"][row, 64:128], start=True, stop=False, tp=(32 * r, half),
             reads=[Z5, A[f"kbv{par4}"]], writes=[PW], tag=tg + "eq")
        b.mm(PW[half:half + 64, 96 * r + 64:96 * r + 96], Z5[row, 0:64],
             A[f"Bq{par}"][row, 32 * r:32 * r + 32], start=False, stop=True,
             tp=(32 * r, half), reads=[Z5, A[f"Bq{par}"]], writes=[PW], tag=tg + "eq")
    hs = slice(half, half + 64)
    ev = PW[hs, 0:384].rearrange("p (r q) -> p r q", q=96)
    ist = A["i64st"][hs, :].rearrange("p (u q) -> p u q", u=1).broadcast_to([64, 4, 64])
    EQt = A[f"EQt{eqp}"]
    b.tt("V", EQt[hs, 0:256].rearrange("p (r q) -> p r q", q=64),
         ev[:, :, 0:64], ist, ALU.add, reads=[PW, A["i64st"]],
         writes=[EQt], tag=tg + "ee")
    b.tt("V", EQt[hs, 256:384].rearrange("p (r q) -> p r q", q=32),
         ev[:, :, 64:96],
         A["SC"][hs, cc, 2, 4 * cg:4 * cg + 4, :], ALU.add,
         reads=[PW, A["SC"]], writes=[EQt], tag=tg + "ee")
    lpb64 = A[f"lamp{wpar}"][hs, cc, 4 * cg:4 * cg + 4].rearrange("p (r u) -> p r u", u=1)\
        .broadcast_to([64, 4, 64])
    lpb32 = A[f"lamp{wpar}"][hs, cc, 4 * cg:4 * cg + 4].rearrange("p (r u) -> p r u", u=1)\
        .broadcast_to([64, 4, 32])
    b.tt("V", A[f"EQ{eqp}"][hs, 0:384].rearrange("p (r q) -> p r q", q=96)[:, :, 0:64],
         EQt[hs, 0:256].rearrange("p (r q) -> p r q", q=64), lpb64,
         ALU.mult, reads=[EQt, A[f"lamp{wpar}"]], writes=[A[f"EQ{eqp}"]],
         tag=tg + "ee")
    b.tt("V", A[f"EQ{eqp}"][hs, 0:384].rearrange("p (r q) -> p r q", q=96)[:, :, 64:96],
         EQt[hs, 256:384].rearrange("p (r q) -> p r q", q=32), lpb32,
         ALU.mult, reads=[EQt, A[f"lamp{wpar}"]], writes=[A[f"EQ{eqp}"]],
         tag=tg + "ee")


def emit_seq(b, A, PS, ch, h, NCH, wk, wpar=0):
    cc, half = h // 2, 64 * (h % 2)
    cg, r = ch // 4, ch % 4
    par4 = h % 4
    eqp = (cg * 16 + h) % 4
    hs = slice(half, half + 64)
    row = slice(32 * r, 32 * r + 32)
    tg = f"{wk}s{ch}h{h}"
    EQ = A[f"EQ{eqp}"]
    Z5 = A[f"Za{par4}"]
    Y = A["Ybuf"][hs, h // 2, :]
    import os as _os9
    if _os9.environ.get("OLDREG"):
        yc = 0
        oc = (64, 128, 64, 128)[par4]
        yk = f"pb3y{par4 % 2}"
        qk = f"pb3q{par4 % 2}"
    else:
        yc = 0 if par4 < 2 else 192
        oc = (64, 128, 256, 320)[par4]
        yk = f"pb3y{par4}"
        qk = f"pb3q{par4}"
    YB = PS[3]
    b.mm(YB[hs, yc:yc + 64], EQ[hs, 96 * r:96 * r + 64], Y, start=True, stop=False,
         tp=(half, half), reads=[EQ, f"Y{h}"], writes=[yk], tag=tg + "E")
    b.mm(YB[hs, yc:yc + 64], A[f"kbv{par4}"][row, 0:64],
         A["Vtm"][row, cg, 64 * h:64 * h + 64], start=False, stop=False,
         tp=(32 * r, half), reads=[A[f"kbv{par4}"], f"Vtm_{cg}_{h}"],
         writes=[yk], tag=tg + "z1")
    b.mm(YB[hs, yc:yc + 64], A[f"kbv{par4}"][row, 64:128], Z5[row, 64:128],
         start=False, stop=True, tp=(32 * r, half),
         reads=[A[f"kbv{par4}"], Z5], writes=[yk], tag=tg + "z2")
    b.mm(YB[32 * r:32 * r + 32, oc:oc + 64],
         EQ[hs, 96 * r + 64:96 * r + 96], Y,
         start=True, stop=True, tp=(half, 32 * r), reads=[EQ, f"Y{h}"],
         writes=[qk], tag=tg + "Q")
    if ch < NCH - 1:
        b.ts("V", A["Ybuf"][hs, h // 2, :], YB[hs, yc:yc + 64], 0.0, None, ALU.add,
             reads=[yk], writes=[f"Y{h}"], tag=tg + "Ye")
    else:
        b.ts("V", A["Ybuf"][hs, h // 2, :], YB[hs, yc:yc + 64],
             A[f"lam{wpar}"][hs, cc, NCH - 1:NCH], None, ALU.mult,
             reads=[yk, A[f"lam{wpar}"]], writes=[f"Y{h}"], tag=tg + "Yl")
    b.tt("V", A["Otm"][32 * r:32 * r + 32, cg, 64 * h:64 * h + 64],
         YB[32 * r:32 * r + 32, oc:oc + 64],
         A[f"OI{par4}"][32 * r:32 * r + 32, :], ALU.add,
         reads=[qk, A[f"OI{par4}"]], writes=[f"Otm_{ch}_{h}"], tag=tg + "Oe")


TAPS = {
    "hw": lambda A: (A["hw"][:], [A["hw"]]),
    "rw": lambda A: (A["rw"][:], [A["rw"]]),
    "kw": lambda A: (A["kw"][:], [A["kw"]]),
    "vw": lambda A: (A["vw"][:], [A["vw"]]),
    "aw": lambda A: (A["aw"][:], [A["aw"]]),
    "logw": lambda A: (A["logw"][:], [A["logw"]]),
    "gw": lambda A: (A["gw"][:], [A["gw"]]),
    "k2": lambda A: (A["k2"][:], [A["k2"]]),
    "kkw": lambda A: (A["kkw"][:], [A["kkw"]]),
    "SC": lambda A: (A["SC"][:], [A["SC"]]),
    "ggate": lambda A: (A["ggate"][:], [A["ggate"]]),
    "beta": lambda A: (A["beta"][:], [A["beta"]]),
    "Vtm": lambda A: (A["Vtm"][:], [A["Vtm"]] + [f"Vtm_{cg}_{h}" for cg in range(2) for h in range(16)]),
    "Otm": lambda A: (A["Otm"][:], [A["Otm"]] + [f"Otm_{ch}_{h}" for ch in range(8) for h in range(16)]),
    "goTM": lambda A: (A["goTM"][:], [A["goTM"]]),
    "attw": lambda A: (A["attw"][:], [A["attw"]]),
    "Tq0": lambda A: (A["Tq0"][:], [A["Tq0"]]),
    "Aq0": lambda A: (A["Aq0"][:], [A["Aq0"]]),
    "rhs0": lambda A: (A["rhs0"][:], [A["rhs0"]]),
    "Za0": lambda A: (A["Za0"][:], [A["Za0"]]),
    "kbv0": lambda A: (A["kbv0"][:], [A["kbv0"]]),
    "EQ0": lambda A: (A["EQ0"][0:64, :], [A["EQ0"]]),
    "scal": lambda A: (A["scal"][:], ["scal"]),
    "smv": lambda A: (A["sm"][:], ["sm"]),
    "smgv": lambda A: (A["smg"][:], ["smg"]),
    "lnf": lambda A: (A["lnf"][:], [A["lnf"]]),
    "upk": lambda A: (A["upk"][:], ["upk0", "upk1"]),
    "gateb": lambda A: (A["gateb"][:], [f"gb{ii}" for ii in range(24)]),
    "xbf": lambda A: (A["xbf"][:], [A["xbf"]]),
    "fout": lambda A: (A["fout"][:], [A["fout"]]),
    "wupg": lambda A: (_FFN_DRAM["wupg"], ["wupT_g"]),
    "mxo": lambda A: (_FFN_DRAM["mx_out_d"], ["mx_out"]),
}
_FFN_DRAM = {}


def ffn_phase(b, A, PS, pT, T, n_cores, NSH, HSH, dram, CT, xres_d, up_sp_d,
              wupT_g, wgtT_g, wdnT_g, mx_in, mx_out, scal_d, y_d, collective,
              ln_apply):
    def bcast128(sm_slot, key_in, sc_slot, key_out, tag):
        b.dma("A", scal_d[sc_slot:sc_slot + 1], A["sm"][0:1, sm_slot:sm_slot + 1],
              reads=[key_in, "sm"], writes=["scal_d"], tag=tag)
        b.dma("G", A["scal"][:, sc_slot:sc_slot + 1],
              scal_d[sc_slot:sc_slot + 1][None, :].broadcast_to([128, 1]),
              reads=["scal_d"], writes=[key_out], tag=tag)
    NB = max(T // 256, 1)
    BS = min(256, T)
    NI = 3072
    NTB = max(T // 512, 1)
    TBS = min(512, T)
    NII = NI // 128
    NCC = 8
    xv = xres_d.rearrange("(cc p) t -> p cc t", p=128)
    SM = lambda i: A["sm"][0:1, i:i + 1]
    SC = lambda i: A["scal"][:, i:i + 1]
    M1L, M1G, S1, C1 = 0, 1, 2, 3
    M2L, M3L, M2G, M3G = 4, 5, 6, 7
    S2, C2, S3, C3 = 8, 9, 10, 11
    M4L, M4G, S4, C4 = 12, 13, 14, 15
    M5L, M5G, S42, S5, C5 = 16, 17, 18, 19, 20
    T1, T2 = 30, 31

    b.dma("S", A["id128f"][:], CT["id128f"].ap()[:], writes=[A["id128f"]], tag="fc")
    b.dma("S", A["magt"][:], CT["magt"].ap()[:], writes=[A["magt"]], tag="fc")
    b.memset("V", A["sm"][:], 0.0, writes=["sm"], tag="fz")
    b.memset("V", A["smg"][:], 0.0, writes=["smg"], tag="fz")
    b.memset("V", A["red1"][:], 0.0, writes=["red1"], tag="fz")
    b.dma("A", mx_in[:], A["sm"][0:1, 0:8], reads=["sm"], writes=["mx_in"], tag="fz")

    for blk in range(NB):
        c0 = blk * BS
        b.dma("G", A["xbf"][:, :, 0:BS], xv[:, :, c0:c0 + BS],
              reads=["xres_dram"], writes=[A["xbf"]], tag="ln2l")
        ln_apply(A["xbf"][:, :, 0:BS], BS, A["lnf"][:, :, c0:c0 + BS],
                 A["sqf"][:, :, 0:BS], A["sc1f"][:, :, 0:BS], None,
                 PS[5][:, 0:BS], PS[4][:, 0:BS], tag=f"ln2_{blk}",
                 stat_name="statf")
    b.red("V", A["red1"][:, 0:1], A["lnf"][:, 0:4, :].rearrange("p a t -> p (a t)"), ALU.max, absval=True,
          reads=[A["lnf"]], writes=["red1"], tag="mx1")
    b.red("V", A["red1"][:, 1:2], A["lnf"][:, 4:8, :].rearrange("p a t -> p (a t)"), ALU.max, absval=True,
          reads=[A["lnf"]], writes=["red1"], tag="mx1")
    b.tt("V", A["red1"][:, 2:3], A["red1"][:, 0:1], A["red1"][:, 1:2], ALU.max,
         reads=["red1", "red1"], writes=["red1"], tag="mx1")

    def xpart_max(col_ap, col_keys, dst_slot, dst_key, tag):
        b.mm(PS[6][0:1, 0:128], col_ap, A["id128f"][:], start=True, stop=True,
             reads=list(col_keys) + [A["id128f"]], writes=[PS[6]], tag=tag)
        b.red("V", SM(dst_slot), PS[6][0:1, 0:128], ALU.max, absval=True,
              reads=[PS[6]], writes=[dst_key], tag=tag)

    def agmax(src0, nsrc, wire0, src_keys, out_slots, out_keys, tag):
        b.dma("A", mx_in[wire0:wire0 + nsrc], A["sm"][0:1, src0:src0 + nsrc],
              reads=list(src_keys), writes=["mx_in"], tag=tag)
        collective(mx_out[:], mx_in[:], reads=["mx_in"], writes=["mx_out"])
        b.dma("S", A["smg"][0:1, 0:8 * n_cores],
              mx_out.rearrange("a b -> (a b)")[None, :],
              reads=["mx_out"], writes=["smg"], tag=tag)
        for j, (dst, dkey) in enumerate(zip(out_slots, out_keys)):
            b.red("V", SM(dst),
                  A["smg"][0:1, 0:8 * n_cores]
                  .rearrange("p (r s) -> p r s", s=8)[:, :, wire0 + j],
                  ALU.max, absval=True, reads=["smg"], writes=[dkey], tag=tag)

    xpart_max(A["red1"][:, 2:3], ["red1"], M1L, "sm", "mx1x")
    agmax(M1L, 1, 0, ["sm"], [M1G], ["sm"], "ag1")
    b.ts("V", SM(S1), SM(M1G), 1.0 / QMAX, 1e-12, ALU.mult, ALU.max,
         reads=["sm"], writes=["sm"], tag="sm")
    b.recip(SM(C1), SM(S1), reads=["sm"], writes=["sm"])
    bcast128(C1, "sm", 0, "scal", "s1b")
    for cc in range(NCC):
        f1 = A["fsf1"] if cc % 2 == 0 else A["fsf1b"]
        b.ts("V", f1[:, 0:T], A["lnf"][:, cc, :], SC(0), MAGIC,
             ALU.mult, ALU.add, reads=[A["lnf"], "scal"], writes=[f1],
             tag="xh")
        b.ts("V", A["lnf"][:, cc, :], f1[:, 0:T], MAGIC, None, ALU.subtract,
             reads=[f1], writes=[A["lnf"]], tag="xh")
    for which, wg_d, gkey in [(0, wupT_g, "u"), (1, wgtT_g, "g")]:
        wgk = "wupT_g" if which == 0 else "wgtT_g"
        for ii in range(NII):
            shd, ioff = ii // NSH, (ii % NSH) * 128
            ws = ii % 2
            b.dma("S", A["ffw"][:, ws, :, :],
                  wg_d[shd].rearrange("(cc p) i -> p cc i", p=128)[:, :, ioff:ioff + 128],
                  reads=[wgk], writes=[f"ffw{ws}"], tag="fwl")
            for tb in range(NTB):
                pb = PS[tb % 4]
                for cc in range(NCC):
                    b.mm(pb[:, 0:TBS], A["ffw"][:, ws, cc, :],
                         A["lnf"][:, cc, TBS * tb:TBS * tb + TBS],
                         start=(cc == 0), stop=(cc == NCC - 1),
                         reads=[f"ffw{ws}", A["lnf"]], writes=[pb], tag="fmm")
                dst = (A["gateb"][:, ii, TBS * tb:TBS * tb + TBS] if which
                       else A["upk"][:, ii % 2, TBS * tb:TBS * tb + TBS])
                b.ts("V", dst, pb[:, 0:TBS], 0.0, None, ALU.add,
                     reads=[pb], writes=[f"{gkey}b{ii}"] + ([] if which else ["upk" + str(ii % 2)]), tag="fme")
            src_buf = A["gateb"][:, ii, :] if which else A["upk"][:, ii % 2, :]
            if which == 0:
                b.dma("A", up_sp_d.rearrange("(jj p) t -> p jj t", p=128)[:, ii, :],
                      A["upk"][:, ii % 2, :], reads=[f"ub{ii}", "upk" + str(ii % 2)], writes=["up_sp"],
                      tag="fsp")
            b.red("V", A["red1"][:, 3:4],
                  src_buf, ALU.max, absval=True, reads=[f"{gkey}b{ii}"],
                  writes=["red1"], tag="fmx")
            if ii == 0:
                b.copy("V", A["red1"][:, 5 + which:6 + which], A["red1"][:, 3:4],
                       reads=["red1"], writes=[f"racc{which}"], tag="fmx")
            else:
                b.tt("V", A["red1"][:, 5 + which:6 + which],
                     A["red1"][:, 5 + which:6 + which], A["red1"][:, 3:4], ALU.max,
                     reads=[f"racc{which}", "red1"], writes=[f"racc{which}"],
                     tag="fmx")
    xpart_max(A["red1"][:, 5:6], ["red1"], M2L, "sm", "mx2x")
    xpart_max(A["red1"][:, 6:7], ["red1"], M3L, "sm", "mx3x")
    agmax(M2L, 2, 1, ["sm", "sm"], [M2G, M3G], ["sm", "sm"], "ag2")
    for mg, mk, s_, c_, sk, ck in [(M2G, "sm", S2, C2, "sm", "sm"),
                                   (M3G, "sm", S3, C3, "sm", "sm")]:
        b.ts("V", SM(T1), SM(mg), SM(S1), 1.0 / QMAX, ALU.mult, ALU.mult,
             reads=[mk, "sm"], writes=["sm"], tag=sk)
        b.ts("V", SM(s_), SM(T1), 1e-12, None, ALU.max,
             reads=["sm"], writes=[sk], tag=sk)
        b.recip(SM(T2), SM(s_), reads=[sk], writes=["sm"])
        b.ts("V", SM(c_), SM(T2), SM(S1), None, ALU.mult,
             reads=["sm", "sm"], writes=[ck], tag=ck)
    bcast128(C2, "sm", 1, "scal", "cb")
    bcast128(C3, "sm", 2, "scal", "cb")
    bcast128(S3, "sm", 3, "scal", "cb")
    for ii in range(NII):
        gb = A["gateb"][:, ii, :]
        f1 = A["fsf1"][:, 0:T] if ii % 2 == 0 else A["fsf1b"][:, 0:T]
        f1k = A["fsf1"] if ii % 2 == 0 else A["fsf1b"]
        u0 = A["upk"][:, 0, 0:T] if ii % 2 == 0 else A["upkb"][:, 0:T]
        u0k = "upk0" if ii % 2 == 0 else A["upkb"]
        b.act(f1, gb, AF.Identity, bias=A["magt"][:, 0:1],
              scale=SC(2), reads=[f"gb{ii}", "scal", A["magt"]],
              writes=[f1k], tag="g3a")
        b.ts("V", u0, f1, MAGIC, None, ALU.subtract,
             reads=[f1k], writes=[u0k], tag="g3b")
        b.act(A["fsf2"][:, 0:T], u0, AF.Sigmoid, scale=SC(3),
              reads=[u0k, "scal"], writes=[A["fsf2"]], tag="g3c")
        b.act(f1, A["fsf2"][:, 0:T], AF.Identity,
              bias=A["magt"][:, 0:1], scale=65536.0,
              reads=[A["fsf2"], A["magt"]], writes=[f1k], tag="g3d")
        b.ts("V", A["fsf2"][:, 0:T], f1, MAGIC, 65535.0,
             ALU.subtract, ALU.min, reads=[f1k], writes=[A["fsf2"]],
             tag="g3e")
        b.tt("G", gb, u0, A["fsf2"][:, 0:T], ALU.mult,
             reads=[u0k, A["fsf2"]], writes=[f"gb{ii}"], tag="g3f")
        b.red("V", A["red1"][:, 3:4], gb, ALU.max, absval=True,
              reads=[f"gb{ii}"], writes=["red1"], tag="g3g")
        if ii == 0:
            b.copy("V", A["red1"][:, 7:8], A["red1"][:, 3:4], reads=["red1"],
                   writes=["red1"], tag="g3h")
        else:
            b.tt("V", A["red1"][:, 7:8], A["red1"][:, 7:8], A["red1"][:, 3:4],
                 ALU.max, reads=["red1", "red1"], writes=["red1"], tag="g3h")
    xpart_max(A["red1"][:, 7:8], ["red1"], M4L, "sm", "mx4x")
    agmax(M4L, 1, 3, ["sm"], [M4G], ["sm"], "ag3")
    b.ts("V", SM(T1), SM(M4G), SM(S3), 1.0 / 65536.0 / QMAX, ALU.mult, ALU.mult,
         reads=["sm", "sm"], writes=["sm"], tag="sm")
    b.ts("V", SM(S4), SM(T1), 1e-12, None, ALU.max, reads=["sm"], writes=["sm"],
         tag="sm")
    b.recip(SM(T2), SM(S4), reads=["sm"], writes=["sm"])
    b.ts("V", SM(C4), SM(T2), SM(S3), 1.0 / 65536.0, ALU.mult, ALU.mult,
         reads=["sm", "sm"], writes=["sm"], tag="sm")
    bcast128(C4, "sm", 4, "scal", "cb")
    for ii in range(NII):
        gb = A["gateb"][:, ii, :]
        f1 = A["fsf1"][:, 0:T] if ii % 2 == 0 else A["fsf1b"][:, 0:T]
        f1k = A["fsf1"] if ii % 2 == 0 else A["fsf1b"]
        u0 = A["upk"][:, 0, 0:T] if ii % 2 == 0 else A["upkb"][:, 0:T]
        u0k = "upk0" if ii % 2 == 0 else A["upkb"]
        b.dma("S", A["upk"][:, 1, 0:T],
              up_sp_d.rearrange("(jj p) t -> p jj t", p=128)[:, ii, :],
              reads=["up_sp"], writes=["upk1"], tag="g4l")
        b.act(f1, gb, AF.Identity, bias=A["magt"][:, 0:1],
              scale=SC(4), reads=[f"gb{ii}", "scal", A["magt"]],
              writes=[f1k], tag="g4a")
        b.ts("V", u0, f1, MAGIC, None, ALU.subtract,
             reads=[f1k], writes=[u0k], tag="g4b")
        b.act(f1, A["upk"][:, 1, 0:T], AF.Identity,
              bias=A["magt"][:, 0:1], scale=SC(1),
              reads=["upk1", "scal", A["magt"]], writes=[f1k], tag="g4c")
        b.ts("V", A["fsf2"][:, 0:T], f1, MAGIC, None, ALU.subtract,
             reads=[f1k], writes=[A["fsf2"]], tag="g4d")
        b.tt("V", gb, u0, A["fsf2"][:, 0:T], ALU.mult,
             reads=[u0k, A["fsf2"]], writes=[f"gb{ii}"], tag="g4e")
        b.red("V", A["red1"][:, 3:4], gb, ALU.max, absval=True,
              reads=[f"gb{ii}"], writes=["red1"], tag="g4f")
        if ii == 0:
            b.copy("V", A["red1"][:, 7:8], A["red1"][:, 3:4], reads=["red1"],
                   writes=["red1"], tag="g4g")
        else:
            b.tt("V", A["red1"][:, 7:8], A["red1"][:, 7:8], A["red1"][:, 3:4],
                 ALU.max, reads=["red1", "red1"], writes=["red1"], tag="g4g")
    xpart_max(A["red1"][:, 7:8], ["red1"], M5L, "sm", "mx5x")
    agmax(M5L, 1, 4, ["sm"], [M5G], ["sm"], "ag4")
    b.ts("V", SM(S42), SM(S4), SM(S2), None, ALU.mult, reads=["sm", "sm"],
         writes=["sm"], tag="sm")
    b.ts("V", SM(T1), SM(M5G), SM(S42), 1.0 / QMAX, ALU.mult, ALU.mult,
         reads=["sm", "sm"], writes=["sm"], tag="sm")
    b.ts("V", SM(S5), SM(T1), 1e-12, None, ALU.max, reads=["sm"], writes=["sm"],
         tag="sm")
    b.recip(SM(T2), SM(S5), reads=["sm"], writes=["sm"])
    b.ts("V", SM(C5), SM(T2), SM(S42), None, ALU.mult, reads=["sm", "sm"],
         writes=["sm"], tag="sm")
    bcast128(C5, "sm", 5, "scal", "cb")
    bcast128(S5, "sm", 6, "scal", "cb")
    for ii in range(NII):
        gb = A["gateb"][:, ii, :]
        f1 = A["fsf1"][:, 0:T] if ii % 2 == 0 else A["fsf1b"][:, 0:T]
        f1k = A["fsf1"] if ii % 2 == 0 else A["fsf1b"]
        b.act(f1, gb, AF.Identity, bias=A["magt"][:, 0:1],
              scale=SC(5), reads=[f"gb{ii}", "scal", A["magt"]],
              writes=[f1k], tag="g5a")
        b.ts("V", gb, f1, MAGIC, None,
             ALU.subtract, reads=[f1k], writes=[f"gb{ii}"], tag="g5b")
    b.barrier()
    TB2 = min(512, T)
    NTB2 = max(T // TB2, 1)
    for hh in range(NCC):
        ws = hh % 2
        b.dma("S", A["dnw"][:, ws, :, :],
              wdnT_g[hh].rearrange("(jj p) h -> p jj h", p=128),
              reads=["wdnT_g"], writes=[f"dnw{ws}"], tag="dwl")
        for ii in range(NII):
            for tb in range(NTB2):
                pb = PS[tb % 4]
                b.mm(pb[:, 0:TB2], A["dnw"][:, ws, ii, :],
                     A["gateb"][:, ii, tb * TB2:(tb + 1) * TB2],
                     start=(ii == 0), stop=(ii == NII - 1),
                     reads=[f"dnw{ws}", f"gb{ii}"], writes=[pb], tag="dmm")
        for tb in range(NTB2):
            b.ts("V", A["lnf"][:, hh, tb * TB2:(tb + 1) * TB2],
                 PS[tb % 4][:, 0:TB2], SC(6), None, ALU.mult,
                 reads=[PS[tb % 4], "scal"], writes=[A["lnf"]], tag="dme")
    for blk in range(NB):
        c0 = blk * BS
        b.dma("S", A["fres"][:, :, 0:BS], xv[:, :, c0:c0 + BS],
              reads=["xres_dram"], writes=[A["fres"]], tag="fl")
        ln_apply(A["lnf"][:, :, c0:c0 + BS], BS, A["fout"][:, :, 0:BS],
                 A["sqf"][:, :, 0:BS], A["sc1f"][:, :, 0:BS],
                 A["fsf1"][:, 0:NCC * BS].rearrange("p (cc t) -> p cc t", t=BS),
                 PS[5][:, 0:BS], PS[4][:, 0:BS], extra=A["fres"][:, :, 0:BS],
                 tag=f"fln{blk}", stat_name="statf")
        b.dma("S", y_d.rearrange("(cc p) t -> p cc t", p=128)[:, :, c0:c0 + BS],
              A["fout"][:, :, 0:BS], reads=[A["fout"]], writes=["y_out"],
              tag="yo")


from concourse.bass_utils import run_bass_kernel_spmd

_NC_CACHE = {}

def _build():
    if "nc" not in _NC_CACHE:
        _NC_CACHE["nc"] = build_program(T=2048, W=128, n_cores=8, taps=(),
                                        upto="full", ffn_prep=True)[0]
    return _NC_CACHE["nc"]


def kernel(**inputs):
    P = {k[len("L0_attn_"):]: np.asarray(v) for k, v in inputs.items()
         if k.startswith("L0_attn_")}
    F = {k[len("L0_ffn_"):]: np.asarray(v) for k, v in inputs.items()
         if k.startswith("L0_ffn_")}
    x = np.ascontiguousarray(P.pop("hidden_states").astype(np.float32, copy=False))
    n_cores = 8
    f32 = np.float32
    vecs = np.ascontiguousarray(np.stack(
        [P["x_r"], P["x_w"], P["x_k"], P["x_v"], P["x_a"], P["k_k"],
         P["k_a"], P["r_k"].reshape(-1), P["w_lora_bias"],
         P["a_lora_bias"], P["g_lora_bias"], P["gn_weight"],
         P["gn_bias"]]).astype(f32).T)
    vrows = np.ascontiguousarray(vecs.T)
    sh = 3072 // n_cores
    shared = {
        "W_r": np.ascontiguousarray(P["W_r"], f32),
        "W_k": np.ascontiguousarray(P["W_k"], f32),
        "W_v": np.ascontiguousarray(P["W_v"], f32),
        "W_o": np.ascontiguousarray(P["W_o"], f32),
        "w_lora_a": np.ascontiguousarray(P["w_lora_a"], f32),
        "w_lora_b": np.ascontiguousarray(P["w_lora_b"], f32),
        "a_lora_a": np.ascontiguousarray(P["a_lora_a"], f32),
        "a_lora_b": np.ascontiguousarray(P["a_lora_b"], f32),
        "g_lora_a": np.ascontiguousarray(P["g_lora_a"], f32),
        "g_lora_b": np.ascontiguousarray(P["g_lora_b"], f32),
        "vecs": vecs, "vrows": vrows,
    }
    wu = np.ascontiguousarray(F["w_up"], f32)
    wg = np.ascontiguousarray(F["w_gate"], f32)
    wd = np.ascontiguousarray(F["w_down"], f32)
    in_maps = []
    for c in range(n_cores):
        m = dict(shared)
        m["xT"] = np.ascontiguousarray(x[c].T)
        m["ffn_up_sh"] = np.ascontiguousarray(wu[c * sh:(c + 1) * sh])
        m["ffn_gate_sh"] = np.ascontiguousarray(wg[c * sh:(c + 1) * sh])
        m["ffn_down_sh"] = np.ascontiguousarray(wd[c * 128:(c + 1) * 128])
        in_maps.append(m)
    nc = _build()
    res = run_bass_kernel_spmd(nc, in_maps, core_ids=list(range(n_cores)))
    out = np.stack([np.ascontiguousarray(r["y"].T) for r in res.results])
    return out.astype(np.float32, copy=False)
